# version 06 (final)
# baseline (speedup 1.0000x reference)
from contextlib import ExitStack

import numpy as np
import concourse.bass as bass
import concourse.mybir as mybir

F32 = mybir.dt.float32
BF16 = mybir.dt.bfloat16
I32 = mybir.dt.int32
U32 = mybir.dt.uint32
ACT = mybir.ActivationFunctionType
ALU = mybir.AluOpType

S, D, DHALF, NST, E, T, KCAP = 4096, 1024, 512, 16, 8, 16384, 4096
EPS = 1e-5
ENGS = ("sync", "vector", "scalar", "tensor", "gpsimd")


class B:

    def __init__(self, nc):
        self.nc = nc
        self.es = ExitStack()
        self.ops = {e: [] for e in ENGS}
        self._sems = {}
        self._barn = 0

    def sem(self, name):
        if name not in self._sems:
            self._sems[name] = [self.es.enter_context(self.nc.semaphore(name)), 0]
        return self._sems[name]

    def n(self, name):
        return self.sem(name)[1]

    def q(self, eng, emit, wait=(), inc=()):
        if not getattr(self, "on", True):
            return
        waits = []
        for s, v in wait:
            h, cnt = self.sem(s)
            waits.append((h, cnt if v is None else v))
        incs = []
        for s, k in inc:
            rec = self.sem(s)
            incs.append((rec[0], k))
            rec[1] += k

        def fn(E, emit=emit, waits=waits, incs=incs):
            for h, v in waits:
                E.wait_ge(h, v)
            ins = emit(E)
            for h, k in incs:
                ins.then_inc(h, k)
        self.ops[eng].append(fn)

    def barrier(self):
        name = f"bar{self._barn}"
        self._barn += 1
        for e in ENGS:
            self.q(e, lambda E: E.nop(), inc=[(name, 1)])
        for e in ENGS:
            self.q(e, lambda E: E.nop(), wait=[(name, 5)])

    def sbuf(self, name, shape, dtype, arena=None):
        tgt = arena if arena is not None else self.es
        return tgt.enter_context(self.nc.sbuf_tensor(name, shape, dtype))

    def psum(self, name, shape, dtype=F32):
        return self.es.enter_context(self.nc.psum_tensor(name, shape, dtype))

    def finish(self):
        with self.nc.Block() as block:
            for e in ENGS:
                lst = self.ops[e]
                if lst:
                    getattr(block, e)(lambda E, lst=lst: [f(E) for f in lst])
        self.es.close()
        return self.nc


def build_L1(upto=99, dbg=False):
    nc = bass.Bass("TRN2", target_bir_lowering=False, detect_race_conditions=False)
    xb = nc.dram_tensor("xb", [S, D], F32, kind="ExternalInput").ap()
    winT = nc.dram_tensor("winT", [D, 1024], F32, kind="ExternalInput").ap()
    convw = nc.dram_tensor("convw", [128, 16], F32, kind="ExternalInput").ap()
    wxT8 = nc.dram_tensor("wxT8", [128, 128], F32, kind="ExternalInput").ap()
    wr8 = nc.dram_tensor("wr8", [128, 64], F32, kind="ExternalInput").ap()
    ident = nc.dram_tensor("ident", [128, 128], BF16, kind="ExternalInput").ap()
    identf = nc.dram_tensor("identf", [128, 128], F32, kind="ExternalInput").ap()
    cst8 = nc.dram_tensor("cst8", [128, 8], F32, kind="ExternalInput").ap()

    st2_out = nc.dram_tensor("st2_out", [2, 128, 256], F32, kind="ExternalOutput").ap()
    stat_row = nc.dram_tensor("stat_row", [2, S], F32, kind="Internal").ap()
    z_out = nc.dram_tensor("z_out", [128, 4 * S], BF16, kind="ExternalOutput").ap()
    xc_out = nc.dram_tensor("xc_out", [128, 4 * S], BF16, kind="ExternalOutput").ap()
    dtbc_out = nc.dram_tensor("dtbc_out", [32, S], F32, kind="ExternalOutput").ap()
    if dbg:
        dbg_z8 = nc.dram_tensor("dbg_z8", [8, 2048], F32, kind="ExternalOutput").ap()
        dbg_zt = nc.dram_tensor("dbg_zt", [128, 128], F32, kind="ExternalOutput").ap()
        dbg_zt2 = nc.dram_tensor("dbg_zt2", [128, 128], F32, kind="ExternalOutput").ap()
        dbg_zmax = nc.dram_tensor("dbg_zmax", [128, 16], F32, kind="ExternalOutput").ap()
        dbg_zsum = nc.dram_tensor("dbg_zsum", [128, 16], F32, kind="ExternalOutput").ap()

    b = B(nc)
    q = b.q

    xnfm = b.sbuf("xnfm", [128, 8 * S], BF16)
    winbf = [b.sbuf(f"winbf{k}", [128, 1024], BF16) for k in range(8)]
    wr_sb = b.sbuf("wr_sb", [128, 64], BF16)
    wx_sb = b.sbuf("wx_sb", [128, 128], BF16)
    cw_sb = b.sbuf("cw_sb", [128, 16], F32)
    id_sb = b.sbuf("id_sb", [128, 128], BF16)
    idf_sb = b.sbuf("idf_sb", [128, 128], F32)
    c8_sb = b.sbuf("c8_sb", [128, 8], F32)
    strow_tmp = b.sbuf("strow_tmp", [32, 128], F32)
    xt = [b.sbuf(f"xt{j}", [128, D], F32) for j in range(2)]
    xbf = [b.sbuf(f"xbf{j}", [128, D], BF16) for j in range(2)]
    ssum = b.sbuf("ssum", [128, 32], F32)
    ssq = b.sbuf("ssq", [128, 32], F32)
    var_pm = b.sbuf("var_pm", [128, 32], F32)
    inv_pm = b.sbuf("inv_pm", [128, 32], F32)
    m_pm = b.sbuf("m_pm", [128, 32], F32)
    inv_bc = b.sbuf("inv_bc", [128, 2048], BF16)
    m_bc = b.sbuf("m_bc", [128, 2048], BF16)
    z8 = b.sbuf("z8", [8, 2048], F32)
    zt = b.sbuf("zt", [128, 16 * 8], F32)
    zmax = b.sbuf("zmax", [128, 16], F32)
    zsum = b.sbuf("zsum", [128, 16], F32)
    zrec = b.sbuf("zrec", [128, 16], F32)
    st_sb = b.sbuf("st_sb", [128, 256], F32)
    x_in = b.sbuf("x_in", [128, 4 * 2051], BF16)
    zhalf = b.sbuf("zhalf", [128, 4 * 2048], BF16)
    xchalf = b.sbuf("xchalf", [128, 4 * 2048], BF16)
    cacc = b.sbuf("cacc", [128, 2048], F32)
    dtbc_sb = b.sbuf("dtbc_sb", [32, 2048], F32)
    ps_tr = [b.psum(f"ps_tr{j}", [128, 128], BF16) for j in range(2)]
    ps_mm = [b.psum(f"ps_mm{j}", [128, 512], F32) for j in range(4)]
    ps_a = b.psum("ps_a", [128, 512], F32)
    ps_sc = ps_a[0:8, :]
    ps_dt = ps_a[0:32, :]
    ps_st = ps_a[0:32, 0:128]
    ps_b = b.psum("ps_b", [128, 16], F32)
    ps_sct = [ps_b[:, 0:8], ps_b[:, 8:16]]

    q("gpsimd", lambda E: E.dma_start(out=wr_sb[:], in_=wr8), inc=[("wld", 16)])
    q("sync", lambda E: E.dma_start(out=cw_sb[:], in_=convw), inc=[("cst", 16)])
    q("sync", lambda E: E.dma_start(out=id_sb[:], in_=ident), inc=[("cst", 16)])
    q("sync", lambda E: E.dma_start(out=idf_sb[:], in_=identf), inc=[("cst", 16)])
    q("sync", lambda E: E.dma_start(out=c8_sb[:], in_=cst8), inc=[("cst", 16)])
    for k in range(8):
        q("gpsimd", lambda E, k=k: E.dma_start(out=winbf[k][:], in_=winT[128 * k:128 * (k + 1), :]),
          inc=[("wld", 16)])
    q("gpsimd", lambda E: E.dma_start(out=wx_sb[:], in_=wxT8), inc=[("wld", 16)])

    for i in range(32):
        w = [("sq", i - 1)] if i >= 2 else []
        q("sync", lambda E, i=i: E.dma_start(out=xt[i % 2][:], in_=xb[128 * i:128 * (i + 1), :]),
          wait=w, inc=[(f"xld{i % 2}", 16)])
        w = [(f"xld{i % 2}", 16 * (i // 2 + 1))]
        if i >= 2:
            w.append(("trp", 8 * (i - 1)))
        q("scalar", lambda E, i=i: E.activation(out=xbf[i % 2][:], in_=xt[i % 2][:],
                                                func=ACT.Identity, accum_out=ssum[:, i:i + 1]),
          wait=w, inc=[("cast", 1)])
        q("scalar", lambda E, i=i: E.activation(out=xt[i % 2][:], in_=xt[i % 2][:],
                                                func=ACT.Square, accum_out=ssq[:, i:i + 1]),
          wait=[("cast", i + 1)], inc=[("sq", 1)])
        for dc in range(8):
            idx = 8 * i + dc
            w = [("cast", i + 1), ("cst", 64)]
            if idx >= 2:
                w.append(("trev", idx - 1))
            q("tensor", lambda E, i=i, dc=dc, idx=idx: E.transpose(
                out=ps_tr[idx % 2][:], in_=xbf[i % 2][:, 128 * dc:128 * (dc + 1)], identity=id_sb[:]),
              wait=w, inc=[("trp", 1)])
            q("vector", lambda E, i=i, dc=dc, idx=idx: E.tensor_copy(
                out=xnfm[:, dc * S + 128 * i: dc * S + 128 * (i + 1)], in_=ps_tr[idx % 2][:]),
              wait=[("trp", idx + 1)], inc=[("trev", 1)])

    q("vector", lambda E: E.tensor_scalar(out=ssum[:], in0=ssum[:], scalar1=1.0 / D, scalar2=None,
                                          op0=ALU.mult), wait=[("sq", 32), ("cast", 32)])
    q("vector", lambda E: E.tensor_scalar(out=ssq[:], in0=ssq[:], scalar1=1.0 / D, scalar2=None,
                                          op0=ALU.mult))
    q("vector", lambda E: E.tensor_tensor(out=var_pm[:], in0=ssum[:], in1=ssum[:], op=ALU.mult))
    q("vector", lambda E: E.tensor_tensor(out=var_pm[:], in0=ssq[:], in1=var_pm[:], op=ALU.subtract),
      inc=[("var", 1)])
    q("scalar", lambda E: E.activation(out=inv_pm[:], in_=var_pm[:], func=ACT.Sqrt, bias=c8_sb[:, 0:1]),
      wait=[("var", 1)], inc=[("sd", 1)])
    q("vector", lambda E: E.reciprocal(out=inv_pm[:], in_=inv_pm[:]),
      wait=[("sd", 1)], inc=[("inv", 1)])
    q("vector", lambda E: E.tensor_tensor(out=m_pm[:], in0=ssum[:], in1=inv_pm[:], op=ALU.mult),
      wait=[("inv", 1)], inc=[("mpm", 1)])
    for r, (src_pm, dep) in enumerate([(inv_pm, "inv"), (m_pm, "mpm")]):
        q("tensor", lambda E, src_pm=src_pm: E.transpose(out=ps_st, in_=src_pm[:], identity=idf_sb[:]),
          wait=[(dep, 1), ("trev", 256)] + ([("trse", 1)] if r else []), inc=[("trs", 1)])
        q("vector", lambda E: E.tensor_copy(out=strow_tmp[:], in_=ps_st),
          wait=[("trs", r + 1)] + ([("srow", 16)] if r else []), inc=[("trse", 1)])
        q("sync", lambda E, r=r: E.dma_start(
            out=stat_row[r:r + 1, :].rearrange("a (i c) -> a (i c)") if False else stat_row[r:r + 1, :],
            in_=strow_tmp[:].rearrange("i c -> (i c)").unsqueeze(0) if False else strow_tmp[:]),
          wait=[("trse", r + 1)], inc=[("srow", 16)])

    b.barrier()

    mmgrp = [0]
    for h in range(2):
        if upto < 2:
            break
        b.on = True
        t0 = 2048 * h
        q("gpsimd", lambda E, t0=t0: E.dma_start(
            out=inv_bc[:], in_=stat_row[0:1, t0:t0 + 2048].broadcast_to([128, 2048])),
          wait=[("srow", 32)], inc=[("bc", 16)])
        q("gpsimd", lambda E, t0=t0: E.dma_start(
            out=m_bc[:], in_=stat_row[1:2, t0:t0 + 2048].broadcast_to([128, 2048])),
          inc=[("bc", 16)])
        for dc in range(8):
            q("vector", lambda E, dc=dc, t0=t0: E.tensor_tensor(
                out=xnfm[:, dc * S + t0: dc * S + t0 + 2048],
                in0=xnfm[:, dc * S + t0: dc * S + t0 + 2048], in1=inv_bc[:], op=ALU.mult),
              wait=[("bc", b.n("bc"))])
            q("vector", lambda E, dc=dc, t0=t0: E.tensor_tensor(
                out=xnfm[:, dc * S + t0: dc * S + t0 + 2048],
                in0=xnfm[:, dc * S + t0: dc * S + t0 + 2048], in1=m_bc[:], op=ALU.subtract),
              inc=[("xn", 1)])
        xn_done = b.n("xn")

        b.on = upto >= 3
        for tc in range(4):
            for k in range(8):
                q("tensor", lambda E, tc=tc, k=k, t0=t0: E.matmul(
                    out=ps_sc, lhsT=wr_sb[:, 8 * k:8 * (k + 1)],
                    rhs=xnfm[:, k * S + t0 + 512 * tc: k * S + t0 + 512 * (tc + 1)],
                    start=(k == 0), stop=(k == 7)),
                  wait=([("xn", xn_done), ("z8e", b.n("z8e")), ("srow", 32), ("wld", 160)] if k == 0 else []),
                  inc=([("scm", 1)] if k == 7 else []))
            q("vector", lambda E, tc=tc: E.tensor_copy(out=z8[:, 512 * tc:512 * (tc + 1)], in_=ps_sc),
              wait=[("scm", b.n("scm"))], inc=[("z8e", 1)])
        if dbg and h == 0:
            q("gpsimd", lambda E: E.dma_start(out=dbg_z8, in_=z8[:]),
              wait=[("z8e", b.n("z8e"))], inc=[("sto", 16)])
        b.on = upto >= 3.2
        for i in range(16):
            w = [("z8e", b.n("z8e"))]
            idx = 16 * h + i
            if idx >= 1:
                w.append(("scte", idx))
            q("tensor", lambda E, i=i, idx=idx: E.transpose(
                out=ps_sct[idx % 2], in_=z8[:, 128 * i:128 * (i + 1)], identity=idf_sb[0:8, 0:8]),
              wait=w, inc=[("sct", 1)])
            q("vector", lambda E, i=i, idx=idx: E.tensor_copy(
                out=zt[:, 8 * i:8 * (i + 1)], in_=ps_sct[idx % 2]),
              wait=[("sct", idx + 1)], inc=[("scte", 1)])
        if dbg and h == 0:
            q("gpsimd", lambda E: E.dma_start(out=dbg_zt, in_=zt[:]),
              wait=[("scte", b.n("scte"))], inc=[("sto", 16)])
        b.on = upto >= 3.4
        for i in range(16):
            q("scalar", lambda E, i=i: E.activation(
                out=zt[:, 8 * i:8 * (i + 1)], in_=zt[:, 8 * i:8 * (i + 1)],
                func=ACT.Exp, accum_out=zsum[:, i:i + 1]),
              wait=[("scte", b.n("scte"))] if i == 0 else [],
              inc=[("zexp", 1)] if i == 15 else [])
        q("vector", lambda E: E.reciprocal(out=zrec[:], in_=zsum[:]),
          wait=[("zexp", b.n("zexp"))])
        q("vector", lambda E: E.drain())
        for i in range(16):
            q("vector", lambda E, i=i: E.scalar_tensor_tensor(
                out=zt[:, 8 * i:8 * (i + 1)], in0=zt[:, 8 * i:8 * (i + 1)],
                scalar=zrec[:, i:i + 1], in1=zt[:, 8 * i:8 * (i + 1)],
                op0=ALU.mult, op1=ALU.bypass),
              inc=[("smax", 1)] if i == 15 else [])
        if dbg and h == 0:
            q("gpsimd", lambda E: E.dma_start(out=dbg_zt2, in_=zt[:]),
              wait=[("smax", b.n("smax"))], inc=[("sto", 16)])
            q("gpsimd", lambda E: E.dma_start(out=dbg_zmax, in_=zmax[:]),
              wait=[("smax", b.n("smax"))], inc=[("sto", 16)])
            q("gpsimd", lambda E: E.dma_start(out=dbg_zsum, in_=zsum[:]),
              wait=[("smax", b.n("smax"))], inc=[("sto", 16)])
        b.on = upto >= 3.6
        for i in range(16):
            q("vector", lambda E, i=i: E.tensor_copy(out=st_sb[:, 16 * i:16 * i + 8],
                                                     in_=zt[:, 8 * i:8 * (i + 1)]),
              wait=([("smax", b.n("smax"))] + ([("sto", b.sem("stst")[1])] if h else []) if i == 0 else []))
            q("vector", lambda E, i=i, h=h: E.tensor_copy(
                out=st_sb[:, 16 * i + 8:16 * i + 9], in_=inv_pm[:, 16 * h + i:16 * h + i + 1]))
            q("vector", lambda E, i=i, h=h: E.tensor_copy(
                out=st_sb[:, 16 * i + 9:16 * i + 10], in_=m_pm[:, 16 * h + i:16 * h + i + 1]))
        b.on = upto >= 3.65
        q("vector", lambda E: E.memset(
            st_sb[:].rearrange("p (i c) -> p i c", c=16)[:, :, 10:16], 0.0))
        b.on = True
        q("vector", lambda E: E.nop(), inc=[("stdone", 1)])
        b.on = upto >= 3.68
        stpre = b.n("sto")
        q("gpsimd", lambda E, h=h: E.dma_start(out=st2_out[h], in_=st_sb[:]),
          wait=[("stdone", b.n("stdone"))], inc=[("sto", 16)])
        b.sem("stst")[1] = b.n("sto")
        b.on = True

        b.on = upto >= 4
        for m in range(8):
            for tc in range(4):
                g = mmgrp[0]
                mmgrp[0] += 1
                pb = ps_mm[g % 4]
                for k in range(8):
                    w = []
                    if k == 0:
                        w = [("xn", xn_done), ("wld", 160)]
                        if g >= 4:
                            w.append(("mmev", g - 3))
                    q("tensor", lambda E, m=m, tc=tc, k=k, pb=pb, t0=t0: E.matmul(
                        out=pb[:], lhsT=winbf[k][:, 128 * m:128 * (m + 1)],
                        rhs=xnfm[:, k * S + t0 + 512 * tc: k * S + t0 + 512 * (tc + 1)],
                        start=(k == 0), stop=(k == 7)),
                      wait=w, inc=([("mm", 1)] if k == 7 else []))
                if m < 4:
                    q("scalar", lambda E, m=m, tc=tc, pb=pb: E.activation(
                        out=x_in[:, m * 2051 + 3 + 512 * tc: m * 2051 + 3 + 512 * (tc + 1)],
                        in_=pb[:], func=ACT.Identity),
                      wait=[("mm", b.n("mm"))], inc=[("mmev", 1)])
                else:
                    q("scalar", lambda E, m=m, tc=tc, pb=pb: E.activation(
                        out=zhalf[:, (m - 4) * 2048 + 512 * tc: (m - 4) * 2048 + 512 * (tc + 1)],
                        in_=pb[:], func=ACT.Silu),
                      wait=[("mm", b.n("mm"))], inc=[("mmev", 1)])
        if h == 0:
            q("vector", lambda E: E.memset(
                x_in[:].rearrange("p (g t) -> p g t", t=2051)[:, :, 0:3], 0.0), inc=[("pad", 1)])
        q("gpsimd", lambda E, t0=t0: E.dma_start(
            out=z_out[:].rearrange("p (g t) -> p g t", t=S)[:, :, t0:t0 + 2048],
            in_=zhalf[:].rearrange("p (g t) -> p g t", t=2048)),
          wait=[("mmev", b.n("mmev"))], inc=[("sto", 16)])

        b.on = upto >= 5
        for g in range(4):
            base = g * 2051
            q("vector", lambda E, base=base, g=g: E.tensor_scalar(
                out=cacc[:], in0=x_in[:, base + 3: base + 3 + 2048],
                scalar1=cw_sb[:, 4 * g + 3:4 * g + 4], scalar2=None, op0=ALU.mult),
              wait=[("mmev", b.n("mmev")), ("pad", 1), ("xcs", b.n("xcs"))])
            for kk in range(3):
                inc = [("cv", 1)] if kk == 2 else []
                q("vector", lambda E, base=base, g=g, kk=kk: E.scalar_tensor_tensor(
                    out=cacc[:], in0=x_in[:, base + kk: base + kk + 2048],
                    scalar=cw_sb[:, 4 * g + kk:4 * g + kk + 1], in1=cacc[:],
                    op0=ALU.mult, op1=ALU.add), inc=inc)
            q("scalar", lambda E, g=g: E.activation(out=xchalf[:, g * 2048:(g + 1) * 2048],
                                                    in_=cacc[:], func=ACT.Silu),
              wait=[("cv", b.n("cv"))], inc=[("xcs", 1)])
        if h == 0:
            q("vector", lambda E: E.tensor_copy(
                out=x_in[:].rearrange("p (g t) -> p g t", t=2051)[:, :, 0:3],
                in_=x_in[:].rearrange("p (g t) -> p g t", t=2051)[:, :, 2048:2051]),
              wait=[("xcs", b.n("xcs"))], inc=[("pad", 1)])
        q("gpsimd", lambda E, t0=t0: E.dma_start(
            out=xc_out[:].rearrange("p (g t) -> p g t", t=S)[:, :, t0:t0 + 2048],
            in_=xchalf[:].rearrange("p (g t) -> p g t", t=2048)),
          wait=[("xcs", b.n("xcs"))], inc=[("sto", 16)])

        b.on = upto >= 6
        for tc in range(4):
            for g in range(4):
                q("tensor", lambda E, tc=tc, g=g: E.matmul(
                    out=ps_dt, lhsT=wx_sb[:, 32 * g:32 * (g + 1)],
                    rhs=xchalf[:, g * 2048 + 512 * tc: g * 2048 + 512 * (tc + 1)],
                    start=(g == 0), stop=(g == 3)),
                  wait=([("xcs", b.n("xcs")), ("dte", b.n("dte")), ("z8e", b.n("z8e"))] if g == 0 else []),
                  inc=([("dm", 1)] if g == 3 else []))
            q("vector", lambda E, tc=tc: E.tensor_copy(out=dtbc_sb[:, 512 * tc:512 * (tc + 1)],
                                                       in_=ps_dt),
              wait=[("dm", b.n("dm"))], inc=[("dte", 1)])
        q("gpsimd", lambda E, t0=t0: E.dma_start(out=dtbc_out[:, t0:t0 + 2048], in_=dtbc_sb[:]),
          wait=[("dte", b.n("dte"))], inc=[("sto", 16)])
        b.on = True
        if h == 0:
            b.barrier()
            q("vector", lambda E: E.nop(), wait=[("sto", b.n("sto"))])
            q("scalar", lambda E: E.nop(), wait=[("sto", b.n("sto"))])

    q("gpsimd", lambda E: E.nop(), wait=[("sto", b.n("sto"))])
    q("sync", lambda E: E.nop(), wait=[("sto", b.n("sto"))])
    return b


def host_L1_inputs(x, W_in, conv_w, W_x, Wr, c):
    bseq, dh = c // 2, c % 2
    ds = slice(512 * dh, 512 * (dh + 1))
    xb = np.ascontiguousarray(x[bseq])
    winT = np.ascontiguousarray(
        np.concatenate([W_in[ds].T, W_in[1024 + 512 * dh:1024 + 512 * (dh + 1)].T], axis=1))
    convw = np.ascontiguousarray(
        conv_w[ds, 0, :].reshape(4, 128, 4).transpose(1, 0, 2).reshape(128, 16))
    wxT8 = np.ascontiguousarray(
        W_x[:, ds].T.reshape(4, 128, 32).transpose(1, 0, 2).reshape(128, 128))
    wr8 = np.ascontiguousarray(Wr.reshape(8, 128, 8).transpose(1, 0, 2).reshape(128, 64))
    import ml_dtypes
    ident = np.eye(128, dtype=ml_dtypes.bfloat16)
    identf = np.eye(128, dtype=np.float32)
    cst8 = np.zeros((128, 8), np.float32)
    cst8[:, 0] = EPS
    cst8[:, 1] = 1.0
    return dict(xb=xb, winT=winT, convw=convw, wxT8=wxT8, wr8=wr8, ident=ident,
                identf=identf, cst8=cst8)



def build_L2(dbg=False, upto=99, nozf=False, nosparse=False):
    nc = bass.Bass("TRN2", target_bir_lowering=False, detect_race_conditions=False)
    import concourse.library_config as libs
    from concourse.bass_core_rust import IndirectOffsetOnAxis

    xfull = nc.dram_tensor("xfull", [T, D], F32, kind="ExternalInput").ap()
    st = nc.dram_tensor("st", [T, 16], F32, kind="ExternalInput").ap()
    xc_in = nc.dram_tensor("xc_in", [128, 4 * S], BF16, kind="ExternalInput").ap()
    z_in = nc.dram_tensor("z_in", [128, 4 * S], BF16, kind="ExternalInput").ap()
    dtbc2 = nc.dram_tensor("dtbc2", [2, 32, S], F32, kind="ExternalInput").ap()
    wdtT = nc.dram_tensor("wdtT", [16, DHALF], F32, kind="ExternalInput").ap()
    bdt8 = nc.dram_tensor("bdt8", [128, 4], F32, kind="ExternalInput").ap()
    alog8 = nc.dram_tensor("alog8", [128, 64], F32, kind="ExternalInput").ap()
    w1 = nc.dram_tensor("w1", [D, 4096], F32, kind="ExternalInput").ap()
    b1l = nc.dram_tensor("b1l", [128, 32], F32, kind="ExternalInput").ap()
    w2 = nc.dram_tensor("w2", [4096, D], F32, kind="ExternalInput").ap()
    b2l = nc.dram_tensor("b2l", [128, 8], F32, kind="ExternalInput").ap()
    iotap = nc.dram_tensor("iotap", [128, 128], F32, kind="ExternalInput").ap()
    ones_row = nc.dram_tensor("ones_row", [1, 128], F32, kind="ExternalInput").ap()
    scflat = nc.dram_tensor("scflat", [128, 128], F32, kind="ExternalInput").ap()
    ohg = nc.dram_tensor("ohg", [128, 512], F32, kind="ExternalInput").ap()
    ident = nc.dram_tensor("ident", [128, 128], BF16, kind="ExternalInput").ap()
    identf = nc.dram_tensor("identf", [128, 128], F32, kind="ExternalInput").ap()
    cst8 = nc.dram_tensor("cst8", [128, 8], F32, kind="ExternalInput").ap()

    gated_out = nc.dram_tensor("gated_out", [128, 4 * S], BF16, kind="ExternalOutput").ap()
    dense = nc.dram_tensor("dense", [T, D], BF16, kind="ExternalOutput").ap()
    if dbg:
        dbg_dt = nc.dram_tensor("dbg_dt", [128, 4 * S], F32, kind="ExternalOutput").ap()
        dbg_y = nc.dram_tensor("dbg_y", [4, 128, S], F32, kind="ExternalOutput").ap()
        dbg_idx = nc.dram_tensor("dbg_idx", [128, 32], I32, kind="ExternalOutput").ap()
        dbg_g = nc.dram_tensor("dbg_g", [128, 32], F32, kind="ExternalOutput").ap()
        dbg_xfm = nc.dram_tensor("dbg_xfm", [128, 1024], BF16, kind="ExternalOutput").ap()
        dbg_th = nc.dram_tensor("dbg_th", [1, 1], F32, kind="ExternalOutput").ap()
        dbg_dtbc = nc.dram_tensor("dbg_dtbc", [32, S], F32, kind="ExternalOutput").ap()
        dbg_dtl = nc.dram_tensor("dbg_dtl", [16, S], BF16, kind="ExternalOutput").ap()
        dbg_cnt = nc.dram_tensor("dbg_cnt", [1, 1], F32, kind="ExternalOutput").ap()

    h_buf = nc.dram_tensor("h_buf", [4096, 4096], BF16, kind="Internal").ap()
    dtbcs = nc.dram_tensor("dtbcs", [32, S], F32, kind="Internal").ap()
    spbuf = nc.dram_tensor("spbuf", [16640, 1], F32, kind="Internal").ap()

    b = B(nc)
    q = b.q

    id_sb = b.sbuf("id_sb", [128, 128], BF16)
    idf_sb = b.sbuf("idf_sb", [128, 128], F32)
    c8_sb = b.sbuf("c8_sb", [128, 8], F32)
    lo_i = b.sbuf("lo_i", [1, 1], F32)
    hi_i = b.sbuf("hi_i", [1, 1], F32)
    mid_i = b.sbuf("mid_i", [1, 1], F32)
    tmp_i = b.sbuf("tmp_i", [1, 1], F32)
    pred_f = b.sbuf("pred_f", [1, 1], F32)
    pred_i = b.sbuf("pred_i", [1, 1], F32)
    mid_bc = b.sbuf("mid_bc", [128, 1], F32)
    th_bci = b.sbuf("th_bci", [128, 1], F32)
    psums_col = b.sbuf("psums_col", [128, 1], F32)
    cnt_sb = b.sbuf("cnt_sb", [1, 1], F32)
    nf_sb = b.sbuf("nf_sb", [1, 1], U32)
    idxf = b.sbuf("idxf", [128, 32], F32)
    idxi = b.sbuf("idxi", [128, 32], I32)
    stg = b.sbuf("stg", [128, 512], F32)
    g_sb = b.sbuf("g_sb", [128, 32], F32)
    inv_g = b.sbuf("inv_g", [128, 32], F32)
    negm_g = b.sbuf("negm_g", [128, 32], F32)
    wdt_sb = b.sbuf("wdt_sb", [16, DHALF], BF16)
    bdt_sb = b.sbuf("bdt_sb", [128, 4], F32)
    A_sb = b.sbuf("A_sb", [128, 64], F32)
    dtl_bf = b.sbuf("dtl_bf", [16, S], BF16)
    b1_sb = b.sbuf("b1_sb", [128, 32], F32)
    b2_sb = b.sbuf("b2_sb", [128, 8], F32)
    ar0 = ExitStack()
    st_sb = b.sbuf("st_sb", [128, 2048], F32, arena=ar0)
    iot_sb = b.sbuf("iot_sb", [128, 128], F32, arena=ar0)
    mask_sb = b.sbuf("mask_sb", [128, 128], F32, arena=ar0)
    dtbc_a = b.sbuf("dtbc_a", [32, S], F32, arena=ar0)
    onesT = b.sbuf("onesT", [128, 128], F32, arena=ar0)
    ones_r = b.sbuf("ones_r", [1, 128], F32, arena=ar0)
    rowc = b.sbuf("rowc", [128, 1], F32, arena=ar0)
    rsum_row = b.sbuf("rsum_row", [1, 128], F32, arena=ar0)
    rcum = b.sbuf("rcum", [1, 128], F32, arena=ar0)
    base_row = b.sbuf("base_row", [1, 128], F32, arena=ar0)
    base_col = b.sbuf("base_col", [128, 1], F32, arena=ar0)
    ecum = b.sbuf("ecum", [128, 128], F32, arena=ar0)
    post = b.sbuf("post", [128, 128], F32, arena=ar0)
    post2 = b.sbuf("post2", [128, 128], F32, arena=ar0)
    posi = b.sbuf("posi", [128, 128], I32, arena=ar0)
    oh_sb = b.sbuf("oh_sb", [128, 512], F32, arena=ar0)
    dtbc_b = b.sbuf("dtbc_b", [32, S], F32, arena=ar0)
    zero2k = b.sbuf("zero2k", [128, 2048], BF16, arena=ar0)

    ps_mm = [b.psum(f"ps_mm{j}", [128, 512], F32) for j in range(4)]
    ps_tr = [b.psum(f"ps_tr{j}", [128, 128], BF16) for j in range(2)]
    ps_f = [b.psum(f"ps_f{j}", [128, 512], F32) for j in range(2)]
    ps_cnt = ps_f[1][0:1, 0:1]

    for t_, src_ in [(id_sb, ident), (idf_sb, identf), (c8_sb, cst8), (iot_sb, iotap),
                     (oh_sb, ohg), (bdt_sb, bdt8), (b1_sb, b1l), (b2_sb, b2l),
                     (dtbc_a, dtbc2[0]), (dtbc_b, dtbc2[1]), (A_sb, alog8)]:
        q("sync", lambda E, t_=t_, src_=src_: E.dma_start(out=t_[:], in_=src_), inc=[("cst", 16)])
    q("sync", lambda E: E.dma_start(out=st_sb[:], in_=st.rearrange("(p j) c -> p (j c)", p=128)),
      inc=[("cst", 16)])
    NCST = 12 * 16
    q("gpsimd", lambda E: E.dma_start(out=wdt_sb[:], in_=wdtT), inc=[("wdtl", 16)])
    q("vector", lambda E: E.memset(zero2k[:], 0.0), inc=[("zz", 1)])
    df = dense.rearrange("a b -> (a b)")
    b.on = not nozf
    for i in range(64):
        q("gpsimd", lambda E, i=i: E.dma_start(
            out=df[262144 * i:262144 * (i + 1)].rearrange("(p w) -> p w", p=128),
            in_=zero2k[:]),
          wait=[("zz", 1)] if i == 0 else [], inc=[("dzf", 16)])

    b.on = upto >= 2
    b.on = True
    sc_view = st_sb[:].rearrange("p (j c) -> p j c", c=16)[:, :, 0:1].rearrange("p j c -> p (j c)")
    scd = b.sbuf("scd", [128, 128], F32, arena=ar0)
    q("sync", lambda E: E.dma_start(out=scd[:], in_=scflat), inc=[("scdc", 16)])
    q("gpsimd", lambda E: E.load_library(libs.attn), wait=[("dzf", b.n("dzf"))] if not nozf else [])
    q("vector", lambda E: E.memset(lo_i[:], 0.0), wait=[("cst", NCST)])
    q("vector", lambda E: E.memset(hi_i[:], 1.0), inc=[("bst", 1)])
    NBIS = 45
    D_ = lambda E: E.drain()
    for it in range(NBIS):
        q("vector", lambda E: E.tensor_tensor(out=mid_i[:], in0=lo_i[:], in1=hi_i[:], op=ALU.add),
          wait=[("bst", b.n("bst"))] if it == 0 else [("bup", b.n("bup"))])
        q("vector", D_)
        q("vector", lambda E: E.tensor_scalar(out=mid_i[:], in0=mid_i[:], scalar1=0.5, scalar2=None,
                                              op0=ALU.mult), inc=[("bmid", 1)])
        q("gpsimd", lambda E: E.partition_broadcast(out_ap=mid_bc[:], in_ap=mid_i[:]),
          wait=[("bmid", b.n("bmid"))], inc=[("bbc2", 1)])
        q("vector", lambda E: E.tensor_scalar(
            out=mask_sb[:], in0=scd[:], scalar1=mid_bc[:, 0:1], scalar2=None,
            op0=ALU.is_ge),
          wait=[("bbc2", b.n("bbc2")), ("scdc", 16)], inc=[("bmask", 1)])
        q("scalar", lambda E: E.activation(out=mask_sb[:], in_=mask_sb[:], func=ACT.Abs,
                                           accum_out=psums_col[:]),
          wait=[("bmask", b.n("bmask"))], inc=[("bacc", 1)])
        q("tensor", lambda E: E.matmul(out=ps_cnt, lhsT=psums_col[:], rhs=c8_sb[:, 1:2],
                                       start=True, stop=True),
          wait=[("bacc", b.n("bacc"))], inc=[("bmm", 1)])
        q("vector", lambda E: E.tensor_copy(out=cnt_sb[:], in_=ps_cnt),
          wait=[("bmm", b.n("bmm"))])
        q("vector", D_)
        q("vector", lambda E: E.tensor_scalar(out=pred_f[:], in0=cnt_sb[:], scalar1=float(KCAP),
                                              scalar2=None, op0=ALU.is_ge))
        q("vector", D_)
        q("vector", lambda E: E.tensor_tensor(out=pred_i[:], in0=pred_f[:], in1=pred_f[:], op=ALU.mult))
        q("vector", D_)
        q("vector", lambda E: E.tensor_tensor(out=tmp_i[:], in0=mid_i[:], in1=lo_i[:], op=ALU.subtract))
        q("vector", D_)
        q("vector", lambda E: E.tensor_tensor(out=tmp_i[:], in0=tmp_i[:], in1=pred_i[:], op=ALU.mult))
        q("vector", D_)
        q("vector", lambda E: E.tensor_tensor(out=lo_i[:], in0=lo_i[:], in1=tmp_i[:], op=ALU.add))
        q("vector", D_)
        q("vector", lambda E: E.tensor_scalar(out=pred_i[:], in0=pred_i[:], scalar1=-1.0, scalar2=None,
                                              op0=ALU.mult))
        q("vector", D_)
        q("vector", lambda E: E.tensor_scalar(out=pred_i[:], in0=pred_i[:], scalar1=1.0, scalar2=None,
                                              op0=ALU.add))
        q("vector", D_)
        q("vector", lambda E: E.tensor_tensor(out=tmp_i[:], in0=hi_i[:], in1=mid_i[:], op=ALU.subtract))
        q("vector", D_)
        q("vector", lambda E: E.tensor_tensor(out=tmp_i[:], in0=tmp_i[:], in1=pred_i[:], op=ALU.mult))
        q("vector", D_)
        q("vector", lambda E: E.tensor_tensor(out=hi_i[:], in0=hi_i[:], in1=tmp_i[:], op=ALU.subtract),
          inc=[("bup", 1)])
    b.on = upto >= 2.5
    q("gpsimd", lambda E: E.partition_broadcast(out_ap=th_bci[:], in_ap=lo_i[:]),
      wait=[("bup", NBIS)], inc=[("thb", 1)])
    q("vector", lambda E: E.tensor_scalar(
        out=mask_sb[:], in0=scd[:], scalar1=th_bci[:, 0:1], scalar2=None,
        op0=ALU.is_ge),
      wait=[("thb", 1)])
    q("vector", lambda E: E.tensor_tensor(out=mask_sb[:], in0=mask_sb[:], in1=mask_sb[:],
                                          op=ALU.mult), inc=[("vb", 1)])
    if dbg:
        q("gpsimd", lambda E: E.dma_start(out=dbg_th, in_=lo_i[:]),
          wait=[("bup", b.n("bup"))], inc=[("sto", 16)])
        q("gpsimd", lambda E: E.dma_start(out=dbg_cnt, in_=cnt_sb[:]), inc=[("sto", 16)])
    b.on = upto >= 3
    q("sync", lambda E: E.dma_start(out=ones_r[:], in_=ones_row), inc=[("onr", 16)])
    q("vector", lambda E: E.memset(onesT[:], 1.0), inc=[("ons", 1)])
    q("scalar", lambda E: E.activation(out=mask_sb[:], in_=mask_sb[:], func=ACT.Identity,
                                       accum_out=rowc[:]),
      wait=[("vb", b.n("vb"))], inc=[("rwc", 1)])
    q("tensor", lambda E: E.transpose(out=ps_f[0][0:1, 0:128], in_=rowc[:], identity=idf_sb[:]),
      wait=[("rwc", b.n("rwc")), ("cst", NCST)], inc=[("rwt", 1)])
    q("vector", lambda E: E.tensor_copy(out=rsum_row[:], in_=ps_f[0][0:1, 0:128]),
      wait=[("rwt", b.n("rwt"))])
    q("vector", lambda E: E.drain())
    q("vector", lambda E: E.tensor_tensor_scan(out=rcum[:], data0=ones_r[:], data1=rsum_row[:],
                                               initial=0.0, op0=ALU.mult, op1=ALU.add),
      wait=[("onr", 16)])
    q("vector", lambda E: E.drain())
    q("vector", lambda E: E.memset(base_row[:, 0:1], 0.0))
    q("vector", lambda E: E.tensor_copy(out=base_row[:, 1:128], in_=rcum[:, 0:127]))
    q("vector", lambda E: E.drain(), inc=[("bsr", 1)])
    q("tensor", lambda E: E.transpose(out=ps_f[1][0:128, 0:1], in_=base_row[:],
                                      identity=idf_sb[0:1, 0:1]),
      wait=[("bsr", b.n("bsr"))], inc=[("bst2", 1)])
    q("vector", lambda E: E.tensor_copy(out=base_col[:], in_=ps_f[1][0:128, 0:1]),
      wait=[("bst2", b.n("bst2"))])
    q("vector", lambda E: E.drain(), inc=[("bcl", 1)])
    q("vector", lambda E: E.tensor_tensor_scan(out=ecum[:], data0=onesT[:], data1=mask_sb[:],
                                               initial=0.0, op0=ALU.mult, op1=ALU.add),
      wait=[("ons", 1)])
    q("vector", lambda E: E.drain())
    q("vector", lambda E: E.tensor_tensor(out=ecum[:], in0=ecum[:], in1=mask_sb[:], op=ALU.subtract))
    q("vector", lambda E: E.scalar_tensor_tensor(out=post[:], in0=ecum[:], scalar=base_col[:, 0:1],
                                                 in1=ecum[:], op0=ALU.add, op1=ALU.bypass),
      wait=[("bcl", 1)])
    q("vector", lambda E: E.tensor_tensor(out=post[:], in0=post[:], in1=mask_sb[:], op=ALU.mult))
    q("vector", lambda E: E.tensor_scalar(out=post2[:], in0=mask_sb[:], scalar1=-16400.0, scalar2=None,
                                          op0=ALU.mult))
    q("vector", lambda E: E.tensor_scalar(out=post2[:], in0=post2[:], scalar1=16400.0, scalar2=None,
                                          op0=ALU.add))
    q("vector", lambda E: E.tensor_tensor(out=post[:], in0=post[:], in1=post2[:], op=ALU.add))
    q("vector", lambda E: E.drain())
    q("vector", lambda E: E.tensor_copy(out=posi[:], in_=post[:]))
    q("vector", lambda E: E.drain(), inc=[("posd", 1)])
    for j in range(128):
        q("gpsimd", lambda E, j=j: E.indirect_dma_start(
            out=spbuf, out_offset=IndirectOffsetOnAxis(posi[:, j:j + 1], 0),
            in_=iot_sb[:, j:j + 1], in_offset=None),
          wait=[("posd", b.n("posd"))] if j == 0 else [], inc=[("spsc", 16)])
    q("sync", lambda E: E.dma_start(
        out=idxf[:], in_=spbuf[0:4096, 0:1].rearrange("(p c) a -> p (c a)", p=128)),
      wait=[("spsc", b.n("spsc"))], inc=[("idxl", 16)])
    q("vector", lambda E: E.tensor_copy(out=idxi[:], in_=idxf[:]),
      wait=[("idxl", 16)], inc=[("idxd", 1)])
    b.on = upto >= 3.6
    for j in range(32):
        q("gpsimd", lambda E, j=j: E.indirect_dma_start(
            out=stg[:, 16 * j:16 * (j + 1)], out_offset=None,
            in_=st, in_offset=IndirectOffsetOnAxis(idxi[:, j:j + 1], 0)),
          wait=[("idxd", b.n("idxd"))] if j == 0 else [], inc=[("stgd", 16)])
    b.on = upto >= 3.8
    q("vector", lambda E: E.tensor_tensor(out=oh_sb[:], in0=stg[:], in1=oh_sb[:], op=ALU.mult),
      wait=[("stgd", b.n("stgd"))], inc=[("ohm", 1)])
    for j in range(32):
        q("scalar", lambda E, j=j: E.activation(out=oh_sb[:, 16 * j:16 * (j + 1)],
                                                in_=oh_sb[:, 16 * j:16 * (j + 1)],
                                                func=ACT.Identity, accum_out=g_sb[:, j:j + 1]),
          wait=[("ohm", 1)] if j == 0 else [], inc=[("gdone", 1)] if j == 31 else [])
        q("vector", lambda E, j=j: E.tensor_copy(out=inv_g[:, j:j + 1], in_=stg[:, 16 * j + 8:16 * j + 9]),
          wait=[("stgd", b.n("stgd"))] if j == 0 else [])
        q("vector", lambda E, j=j: E.tensor_copy(out=negm_g[:, j:j + 1], in_=stg[:, 16 * j + 9:16 * j + 10]))
    q("vector", lambda E: E.drain())
    q("vector", lambda E: E.tensor_scalar(out=negm_g[:], in0=negm_g[:], scalar1=-1.0, scalar2=None,
                                          op0=ALU.mult), inc=[("invd", 1)])
    if dbg:
        q("gpsimd", lambda E: E.dma_start(out=dbg_idx, in_=idxi[:]), wait=[("idxd", 1)], inc=[("sto", 16)])
        q("gpsimd", lambda E: E.dma_start(out=dbg_g, in_=g_sb[:]), wait=[("gdone", 1)], inc=[("sto", 16)])

    b.on = upto >= 4
    q("vector", lambda E: E.tensor_tensor(out=dtbc_a[:], in0=dtbc_a[:], in1=dtbc_b[:], op=ALU.add),
      wait=[("cst", NCST)], inc=[("dtal", 1)])
    q("sync", lambda E: E.dma_start(out=dtbcs, in_=dtbc_a[:]), wait=[("dtal", 1)], inc=[("dtbs", 16)])
    q("vector", lambda E: E.tensor_copy(out=dtl_bf[:], in_=dtbc_a[0:16, :]),
      wait=[("dtal", 1)], inc=[("dtlc", 1)])
    q("scalar", lambda E: E.activation(out=A_sb[:], in_=A_sb[:], func=ACT.Exp),
      wait=[("cst", NCST)], inc=[("aexp", 1)])
    q("vector", lambda E: E.tensor_scalar(out=A_sb[:], in0=A_sb[:], scalar1=-1.0, scalar2=None,
                                          op0=ALU.mult), wait=[("aexp", 1)], inc=[("adone", 1)])

    b.on = True
    b.barrier()
    for e in ENGS:
        q(e, lambda E: E.nop(), wait=[(s_, b.n(s_)) for s_ in
                                      ("dzf", "sg", "sgte", "gdone", "stgd", "v16d", "dtbs", "dtlc")])
    ar0.close()

    ar1 = ExitStack()
    dt_sb = b.sbuf("dt_sb", [128, 4 * S], F32, arena=ar1)
    xc_sb = b.sbuf("xc_sb", [128, 4 * S], BF16, arena=ar1)
    sp_tmp = b.sbuf("sp_tmp", [128, 512], F32, arena=ar1)
    sp_tmp2 = b.sbuf("sp_tmp2", [128, 512], F32, arena=ar1)
    sp_v = b.sbuf("sp_v", [128, 512], F32, arena=ar1)
    a_t = [b.sbuf(f"a_t{j}", [128, S], F32, arena=ar1) for j in range(2)]
    u_t = b.sbuf("u_t", [128, S], BF16, arena=ar1)
    h_t = b.sbuf("h_t", [128, S], BF16, arena=ar1)
    y_gt = b.sbuf("y_gt", [128, S], F32, arena=ar1)
    bbc = [b.sbuf(f"bbc{j}", [128, S], BF16, arena=ar1) for j in range(2)]

    b.on = upto >= 4
    q("sync", lambda E: E.dma_start(out=xc_sb[:], in_=xc_in), inc=[("xcl", 16)])
    if dbg:
        q("gpsimd", lambda E: E.dma_start(out=dbg_dtbc, in_=dtbc_a[:]),
          wait=[("dtal", 1)], inc=[("sto", 16)])
        q("gpsimd", lambda E: E.dma_start(out=dbg_dtl, in_=dtl_bf[:]),
          wait=[("dtlc", 1)], inc=[("sto", 16)])
    for g in range(4):
        for tc in range(8):
            gi = g * 8 + tc
            pb = ps_mm[gi % 4]
            w = [("dtlc", 1), ("wdtl", 16)]
            if gi >= 4:
                w.append(("spev", gi - 3))
            q("tensor", lambda E, g=g, tc=tc, pb=pb: E.matmul(
                out=pb[:], lhsT=wdt_sb[:, 128 * g:128 * (g + 1)],
                rhs=dtl_bf[:, 512 * tc:512 * (tc + 1)], start=True, stop=True),
              wait=w, inc=[("dtm", 1)])
            q("scalar", lambda E, g=g, tc=tc, pb=pb: E.activation(
                out=sp_v[:], in_=pb[:], func=ACT.Identity, bias=bdt_sb[:, g:g + 1]),
              wait=[("dtm", b.n("dtm"))] + ([("spev", gi)] if gi >= 1 else []))
            q("scalar", lambda E, g=g, tc=tc: E.activation(
                out=dt_sb[:, g * S + 512 * tc: g * S + 512 * (tc + 1)], in_=sp_v[:],
                func=ACT.Relu))
            q("scalar", lambda E: E.activation(
                out=sp_tmp[:], in_=sp_v[:], func=ACT.Abs))
            q("scalar", lambda E: E.activation(out=sp_tmp2[:], in_=sp_tmp[:], func=ACT.Exp,
                                               scale=c8_sb[:, 2:3]))
            q("scalar", lambda E: E.activation(out=sp_tmp[:], in_=sp_tmp2[:], func=ACT.Ln,
                                               bias=c8_sb[:, 1:2]), inc=[("spln", 1)])
            q("vector", lambda E, g=g, tc=tc: E.tensor_tensor(
                out=dt_sb[:, g * S + 512 * tc: g * S + 512 * (tc + 1)],
                in0=dt_sb[:, g * S + 512 * tc: g * S + 512 * (tc + 1)], in1=sp_tmp[:], op=ALU.add),
              wait=[("spln", b.n("spln"))], inc=[("spev", 1)])
    if dbg:
        q("gpsimd", lambda E: E.dma_start(out=dbg_dt, in_=dt_sb[:]),
          wait=[("spev", 32)], inc=[("sto", 16)])

    b.on = upto >= 5
    for g in range(4):
        q("vector", lambda E: E.memset(y_gt[:], 0.0),
          wait=[("gst", b.n("gst"))], inc=[("ygz", 1)])
        for n in range(NST):
            it = g * NST + n
            q("gpsimd", lambda E, n=n, it=it: E.dma_start(
                out=bbc[it % 2][:], in_=dtbcs[16 + n:17 + n, :].broadcast_to([128, S])),
              wait=[("dtbs", 16)] + ([("umul", it - 1)] if it >= 2 else []),
              inc=[(f"bbd{it % 2}", 16)])
            b.on = upto >= 5.3
            q("scalar", lambda E, g=g, n=n, it=it: E.activation(
                out=a_t[it % 2][:], in_=dt_sb[:, g * S:(g + 1) * S], func=ACT.Exp,
                scale=A_sb[:, 16 * g + n:16 * g + n + 1]),
              wait=[("spev", 32), ("adone", 1)] + ([("scn", it - 1)] if it >= 2 else []),
              inc=[("aet", 1)])
            q("vector", lambda E, g=g, it=it: E.tensor_tensor(
                out=u_t[:], in0=bbc[it % 2][:],
                in1=xc_sb[:, g * S:(g + 1) * S], op=ALU.mult),
              wait=[(f"bbd{it % 2}", 16 * (it // 2 + 1)), ("xcl", 16), ("ygz", b.n("ygz"))],
              inc=[("umul", 1)])
            b.on = upto >= 5.5
            q("vector", lambda E, it=it: E.tensor_tensor_scan(
                out=h_t[:], data0=a_t[it % 2][:], data1=u_t[:], initial=0.0,
                op0=ALU.mult, op1=ALU.add),
              wait=[("aet", it + 1)], inc=[("scn", 1)])
            b.on = upto >= 5.5
            q("vector", lambda E: E.tensor_tensor(out=y_gt[:], in0=y_gt[:], in1=h_t[:], op=ALU.add),
              inc=[("yad", 1)])
        b.on = upto >= 5.7
        q("gpsimd", lambda E, g=g: E.dma_start(out=bbc[0][:], in_=z_in[:, g * S:(g + 1) * S]),
          wait=[("gst", b.n("gst")), ("umul", b.n("umul"))], inc=[(f"zgl{g % 2}", 16)])
        q("vector", lambda E, g=g: E.tensor_tensor(out=bbc[0][:], in0=y_gt[:], in1=bbc[0][:], op=ALU.mult),
          wait=[(f"zgl{g % 2}", 16 * (g // 2 + 1))], inc=[("gmul", 1)])
        if dbg:
            q("gpsimd", lambda E, g=g: E.dma_start(out=dbg_y[g], in_=y_gt[:]),
              wait=[("yad", b.n("yad"))], inc=[("sto", 16)])
        q("gpsimd", lambda E, g=g: E.dma_start(out=gated_out[:, g * S:(g + 1) * S], in_=bbc[0][:]),
          wait=[("gmul", b.n("gmul"))], inc=[("gst", 16)])

    b.on = True
    b.barrier()
    for e in ENGS:
        q(e, lambda E: E.nop(), wait=[("gst", b.n("gst")), ("sto", b.n("sto"))])
    ar1.close()

    ar2 = ExitStack()
    xg_bf = [b.sbuf(f"xg_bf{j}", [128, D], BF16, arena=ar2) for j in range(2)]
    xn_row = [b.sbuf(f"xn_row{j}", [128, D], BF16, arena=ar2) for j in range(2)]
    xfm = b.sbuf("xfm", [128, 8 * 4096], BF16, arena=ar2)
    wbig = [b.sbuf(f"wbig{k}", [128, 4096], BF16, arena=ar2) for k in range(8)]
    h_sb = b.sbuf("h_sb", [128, 4 * 4096], BF16, arena=ar2)
    hrd = [b.sbuf(f"hrd{j}", [128, 512], BF16, arena=ar2) for j in range(4)]
    xe_sb = b.sbuf("xe_sb", [128, 8 * 512], BF16, arena=ar2)
    ctrb = [b.sbuf(f"ctrb{j}", [128, D], BF16, arena=ar2) for j in range(2)]

    b.on = upto >= 6
    for j in range(32):
        par = j % 2
        w = [("idxd", 1)] if j < 2 else [("xaf", j - 1)]
        q("gpsimd", lambda E, j=j, par=par: E.indirect_dma_start(
            out=xg_bf[par][:], out_offset=None,
            in_=xfull, in_offset=IndirectOffsetOnAxis(idxi[:, j:j + 1], 0)),
          wait=w, inc=[(f"xgd{par}", 16)])
        w = [(f"xgd{par}", 16 * (j // 2 + 1)), ("invd", 1)]
        if j >= 2:
            w.append(("xte", 8 * (j - 1)))
        q("scalar", lambda E, j=j, par=par: E.activation(
            out=xn_row[par][:], in_=xg_bf[par][:], func=ACT.Identity,
            scale=inv_g[:, j:j + 1], bias=negm_g[:, j:j + 1]),
          wait=w, inc=[("xaf", 1)])
        for dc in range(8):
            idx2 = 8 * j + dc
            wv = [("xaf", j + 1)]
            if idx2 >= 2:
                wv.append(("xte", idx2 - 1))
            q("tensor", lambda E, j=j, dc=dc, par=par, idx2=idx2: E.transpose(
                out=ps_tr[idx2 % 2][:], in_=xn_row[par][:, 128 * dc:128 * (dc + 1)], identity=id_sb[:]),
              wait=wv, inc=[("xtp", 1)])
            q("vector", lambda E, j=j, dc=dc, idx2=idx2: E.tensor_copy(
                out=xfm[:, dc * 4096 + 128 * j: dc * 4096 + 128 * (j + 1)], in_=ps_tr[idx2 % 2][:]),
              wait=[("xtp", idx2 + 1)], inc=[("xte", 1)])
    if dbg:
        q("gpsimd", lambda E: E.dma_start(out=dbg_xfm, in_=xfm[:, 0:1024]),
          wait=[("xte", 256)], inc=[("sto", 16)])

    b.on = upto >= 7
    for k in range(8):
        q("gpsimd", lambda E, k=k: E.dma_start(out=wbig[k][:], in_=w1[128 * k:128 * (k + 1), :]),
          inc=[("w1l", 16)])

    for H in range(8):
        for m in range(4):
            for tc in range(8):
                gi = (H * 4 + m) * 8 + tc
                pb = ps_mm[gi % 4]
                for k in range(8):
                    w = []
                    if k == 0:
                        w = [("xte", 256), ("w1l", 128)]
                        if gi >= 4:
                            w.append(("mm1e", gi - 3))
                    q("tensor", lambda E, H=H, m=m, tc=tc, k=k, pb=pb: E.matmul(
                        out=pb[:], lhsT=wbig[k][:, 512 * H + 128 * m: 512 * H + 128 * (m + 1)],
                        rhs=xfm[:, k * 4096 + 512 * tc: k * 4096 + 512 * (tc + 1)],
                        start=(k == 0), stop=(k == 7)),
                      wait=w, inc=[("mm1", 1)] if k == 7 else [])
                q("scalar", lambda E, H=H, m=m, tc=tc, pb=pb: E.activation(
                    out=h_sb[:, m * 4096 + 512 * tc: m * 4096 + 512 * (tc + 1)], in_=pb[:],
                    func=ACT.Gelu, bias=b1_sb[:, H * 4 + m:H * 4 + m + 1]),
                  wait=[("mm1", b.n("mm1"))] + ([("hbs", H)] if (m == 0 and tc == 0 and H >= 1) else []),
                  inc=[("mm1e", 1)])
        q("gpsimd", lambda E, H=H: E.dma_start(
            out=h_buf[512 * H:512 * (H + 1), :].rearrange("(m p) t -> p m t", m=4),
            in_=h_sb[:].rearrange("p (m t) -> p m t", m=4)),
          wait=[("mm1e", b.n("mm1e"))], inc=[("hbs0", 16)])
        q("gpsimd", lambda E: E.nop(), wait=[("hbs0", b.n("hbs0"))], inc=[("hbs", 1)])

    for kt in range(32):
        q("gpsimd", lambda E, kt=kt: E.dma_start(
            out=wbig[kt // 4][:, 1024 * (kt % 4):1024 * (kt % 4 + 1)],
            in_=w2[128 * kt:128 * (kt + 1), :]),
          wait=[("mm1", b.n("mm1"))] if kt == 0 else [], inc=[("w2l", 16)])

    b.on = upto >= 8
    for ts in range(8):
        for half in range(2):
            gh = ts * 2 + half
            for kt in range(32):
                gi = gh * 32 + kt
                par = gi % 4
                w = [("hbs", 8), ("w2l", 512)] if gi < 4 else [("mm2", gi - 3)]
                q("sync", lambda E, ts=ts, kt=kt, par=par: E.dma_start(
                    out=hrd[par][:], in_=h_buf[128 * kt:128 * (kt + 1), 512 * ts:512 * (ts + 1)]),
                  wait=w, inc=[(f"hr{par}", 16)])
                for mi in range(4):
                    md = half * 4 + mi
                    w2_ = [(f"hr{par}", 16 * (gi // 4 + 1))]
                    if kt == 0 and mi == 0 and gh >= 1:
                        w2_.append(("mm2e", 4 * gh))
                    q("tensor", lambda E, ts=ts, kt=kt, md=md, mi=mi, par=par: E.matmul(
                        out=ps_mm[mi][:],
                        lhsT=wbig[kt // 4][:, 1024 * (kt % 4) + 128 * md: 1024 * (kt % 4) + 128 * (md + 1)],
                        rhs=hrd[par][:], start=(kt == 0), stop=(kt == 31)),
                      wait=w2_ if mi == 0 else [],
                      inc=[("mm2", 1)] if mi == 3 else [])
            for mi in range(4):
                md = half * 4 + mi
                w3 = [("mm2", b.n("mm2"))]
                if ts >= 1 and mi == 0:
                    w3.append(("cte", 32 * ts))
                q("scalar", lambda E, ts=ts, md=md, mi=mi: E.activation(
                    out=xe_sb[:, 512 * md:512 * (md + 1)], in_=ps_mm[mi][:],
                    func=ACT.Identity, bias=b2_sb[:, md:md + 1]),
                  wait=w3, inc=[("mm2e", 1)])
        for tt in range(4):
            j = 4 * ts + tt
            cpar = j % 2
            for md in range(8):
                idx3 = j * 8 + md
                w3 = [("mm2e", b.n("mm2e"))]
                if idx3 >= 2:
                    w3.append(("cte", idx3 - 1))
                q("tensor", lambda E, md=md, tt=tt, idx3=idx3: E.transpose(
                    out=ps_tr[idx3 % 2][:], in_=xe_sb[:, 512 * md + 128 * tt: 512 * md + 128 * (tt + 1)],
                    identity=id_sb[:]),
                  wait=w3, inc=[("ctp", 1)])
                q("scalar", lambda E, j=j, md=md, idx3=idx3, cpar=cpar: E.activation(
                    out=ctrb[cpar][:, 128 * md:128 * (md + 1)], in_=ps_tr[idx3 % 2][:],
                    func=ACT.Identity, scale=g_sb[:, j:j + 1]),
                  wait=[("ctp", idx3 + 1)] + ([("csc" + str(cpar), 16 * (j // 2))] if j >= 2 else []),
                  inc=[("cte", 1)])
            q("gpsimd", lambda E, j=j, cpar=cpar: E.indirect_dma_start(
                out=dense, out_offset=IndirectOffsetOnAxis(idxi[:, j:j + 1], 0),
                in_=ctrb[cpar][:], in_offset=None),
              wait=[("cte", b.n("cte")), ("dzf", b.n("dzf"))], inc=[("csc" + str(cpar), 16)])

    b.on = True
    q("gpsimd", lambda E: E.nop(), wait=[("csc0", b.n("csc0")), ("csc1", b.n("csc1")),
                                         ("gst", b.n("gst")), ("sto", b.n("sto"))])
    q("sync", lambda E: E.nop(), wait=[("csc0", b.n("csc0")), ("csc1", b.n("csc1"))])
    q("vector", lambda E: E.nop(), wait=[("csc0", b.n("csc0")), ("csc1", b.n("csc1"))])
    ar2.close()
    return b


def host_L2_inputs(x, st_full, xc_dev, z_dev, dtbc_all, W_dt, b_dt, A_log, W1, b1, W2, b2, c):
    bseq, dh, e = c // 2, c % 2, c
    ds = slice(512 * dh, 512 * (dh + 1))
    import ml_dtypes
    partner = c ^ 1
    iotap = (128.0 * np.arange(128)[:, None] + np.arange(128)[None, :]).astype(np.float32)
    ohg = np.zeros((128, 512), np.float32)
    ohg[:, e::16] = 1.0
    cst8 = np.zeros((128, 8), np.float32)
    cst8[:, 0] = EPS; cst8[:, 1] = 1.0; cst8[:, 2] = -1.0
    return dict(
        xfull=np.ascontiguousarray(x.reshape(T, D)),
        st=st_full,
        xc_in=xc_dev, z_in=z_dev,
        dtbc2=np.stack([dtbc_all[c], dtbc_all[partner]]),
        wdtT=np.ascontiguousarray(W_dt[ds, :].T),
        bdt8=np.ascontiguousarray(b_dt[ds].reshape(4, 128).T),
        alog8=np.ascontiguousarray(A_log[ds].reshape(4, 128, 16).transpose(1, 0, 2).reshape(128, 64)),
        w1=np.ascontiguousarray(W1[e]), b1l=np.ascontiguousarray(b1[e].reshape(8, 4, 128).transpose(2, 0, 1).reshape(128, 32)),
        w2=np.ascontiguousarray(W2[e]), b2l=np.ascontiguousarray(b2[e].reshape(8, 128).T),
        iotap=iotap, ohg=ohg, ones_row=np.ones((1, 128), np.float32),
        scflat=np.ascontiguousarray(st_full[:, e].reshape(128, 128)),
        ident=np.eye(128, dtype=ml_dtypes.bfloat16), identf=np.eye(128, dtype=np.float32),
        cst8=cst8,
    )

def build_L3():
    nc = bass.Bass("TRN2", target_bir_lowering=False, detect_race_conditions=False)
    xw = nc.dram_tensor("xw", [2048, D], F32, kind="ExternalInput").ap()
    gated2 = nc.dram_tensor("gated2", [2, 128, 4 * 2048], BF16, kind="ExternalInput").ap()
    dwin = nc.dram_tensor("dwin", [8, 2048, D], BF16, kind="ExternalInput").ap()
    woT = nc.dram_tensor("woT", [D, D], F32, kind="ExternalInput").ap()
    ident = nc.dram_tensor("ident", [128, 128], BF16, kind="ExternalInput").ap()
    out = nc.dram_tensor("out", [2048, D], F32, kind="ExternalOutput").ap()

    b = B(nc)
    q = b.q
    id_sb = b.sbuf("id_sb", [128, 128], BF16)
    wo_sb = [b.sbuf(f"wo{k}", [128, 1024], BF16) for k in range(8)]
    gt_sb = b.sbuf("gt_sb", [128, 8 * 2048], BF16)
    att_sb = b.sbuf("att_sb", [128, 8 * 2048], BF16)
    acc = [b.sbuf(f"acc{j}", [128, 1024], F32) for j in range(2)]
    dw_t = [b.sbuf(f"dw{j}", [128, 1024], BF16) for j in range(4)]
    xw_t = [b.sbuf(f"xwt{j}", [128, 1024], F32) for j in range(2)]
    out_t = [b.sbuf(f"outt{j}", [128, 1024], F32) for j in range(2)]
    ps_mm = [b.psum(f"ps_mm{j}", [128, 512], F32) for j in range(4)]
    ps_tr = [b.psum(f"ps_tr{j}", [128, 128], BF16) for j in range(2)]

    q("sync", lambda E: E.dma_start(out=id_sb[:], in_=ident), inc=[("cst", 16)])
    for k in range(8):
        q("gpsimd", lambda E, k=k: E.dma_start(out=wo_sb[k][:], in_=woT[128 * k:128 * (k + 1), :]),
          inc=[("wol", 16)])
    for half in range(2):
        q("sync", lambda E, half=half: E.dma_start(
            out=gt_sb[:, half * 4 * 2048:(half + 1) * 4 * 2048], in_=gated2[half]),
          inc=[("gtl", 16)])
    for m in range(8):
        for tc in range(4):
            gi = m * 4 + tc
            pb = ps_mm[gi % 4]
            for kc in range(8):
                w = []
                if kc == 0:
                    w = [("gtl", 32), ("wol", 128), ("cst", 16)]
                    if gi >= 4:
                        w.append(("ate", gi - 3))
                q("tensor", lambda E, m=m, tc=tc, kc=kc, pb=pb: E.matmul(
                    out=pb[:], lhsT=wo_sb[kc][:, 128 * m:128 * (m + 1)],
                    rhs=gt_sb[:, kc * 2048 + 512 * tc: kc * 2048 + 512 * (tc + 1)],
                    start=(kc == 0), stop=(kc == 7)),
                  wait=w, inc=[("atm", 1)] if kc == 7 else [])
            q("scalar", lambda E, m=m, tc=tc, pb=pb: E.activation(
                out=att_sb[:, m * 2048 + 512 * tc: m * 2048 + 512 * (tc + 1)], in_=pb[:],
                func=ACT.Identity),
              wait=[("atm", b.n("atm"))], inc=[("ate", 1)])
    for tt in range(16):
        par = tt % 2
        q("sync", lambda E, tt=tt, par=par: E.dma_start(
            out=xw_t[par][:], in_=xw[128 * tt:128 * (tt + 1), :]),
          wait=[("acc0", tt - 1)] if tt >= 2 else [], inc=[(f"xwl{par}", 16)])
        q("vector", lambda E, tt=tt, par=par: E.tensor_copy(out=acc[par][:], in_=xw_t[par][:]),
          wait=[(f"xwl{par}", 16 * (tt // 2 + 1))], inc=[("acc0", 1)])
        for e in range(8):
            gdi = 8 * tt + e
            q("gpsimd", lambda E, tt=tt, e=e, gdi=gdi: E.dma_start(
                out=dw_t[gdi % 4][:], in_=dwin[e, 128 * tt:128 * (tt + 1), :]),
              wait=[("acca", gdi - 3)] if gdi >= 4 else [], inc=[(f"dwl{gdi % 4}", 16)])
            q("vector", lambda E, tt=tt, e=e, par=par, gdi=gdi: E.tensor_tensor(
                out=acc[par][:], in0=acc[par][:], in1=dw_t[gdi % 4][:], op=ALU.add),
              wait=[(f"dwl{gdi % 4}", 16 * (gdi // 4 + 1)), ("acc0", b.n("acc0"))],
              inc=[("acca", 1)])
        for m in range(8):
            idx4 = tt * 8 + m
            w4 = [("ate", 32)]
            if idx4 >= 2:
                w4.append(("atte", idx4 - 1))
            q("tensor", lambda E, tt=tt, m=m, idx4=idx4: E.transpose(
                out=ps_tr[idx4 % 2][:], in_=att_sb[:, m * 2048 + 128 * tt: m * 2048 + 128 * (tt + 1)],
                identity=id_sb[:]),
              wait=w4, inc=[("attp", 1)])
            q("vector", lambda E, tt=tt, m=m, idx4=idx4, par=par: E.tensor_tensor(
                out=acc[par][:, 128 * m:128 * (m + 1)], in0=acc[par][:, 128 * m:128 * (m + 1)],
                in1=ps_tr[idx4 % 2][:], op=ALU.add),
              wait=[("attp", idx4 + 1), ("acca", b.n("acca"))], inc=[("atte", 1)])
        q("vector", lambda E, par=par: E.tensor_copy(out=out_t[par][:], in_=acc[par][:]),
          wait=[("atte", b.n("atte"))], inc=[("outc", 1)])
        q("gpsimd", lambda E, tt=tt, par=par: E.dma_start(
            out=out[128 * tt:128 * (tt + 1), :], in_=out_t[par][:]),
          wait=[("outc", b.n("outc"))] + ([("outs", tt - 2 + 1)] if tt >= 2 else []),
          inc=[("outs", 16)])
    q("gpsimd", lambda E: E.nop(), wait=[("outs", 16 * 16)])
    q("sync", lambda E: E.nop(), wait=[("outs", 16 * 16)])
    return b


def host_L3_inputs(x, gated_all, dense_all, W_out, c):
    import ml_dtypes
    bq, dh = c // 2, c % 2
    w0 = 2048 * c
    xw = np.ascontiguousarray(x.reshape(T, D)[w0:w0 + 2048])
    g2 = np.zeros((2, 128, 4 * 2048), dtype=ml_dtypes.bfloat16)
    for half in range(2):
        src = np.asarray(gated_all[2 * bq + half]).reshape(128, 4, 4096)
        g2[half] = np.ascontiguousarray(src[:, :, 2048 * dh:2048 * (dh + 1)]).reshape(128, 4 * 2048)
    dwin = np.stack([np.asarray(dense_all[e])[w0:w0 + 2048] for e in range(8)])
    woT = np.ascontiguousarray(W_out.T)
    return dict(xw=xw, gated2=g2, dwin=np.ascontiguousarray(dwin), woT=woT,
                ident=np.eye(128, dtype=ml_dtypes.bfloat16))


_CACHE = {}

def _get(name, builder):
    if name not in _CACHE:
        b = builder()
        nc = b.finish()
        from concourse.library_overlay import lower_extended_insts
        lower_extended_insts(nc)
        _CACHE[name] = nc
    return _CACHE[name]


def kernel(L0_attn_x, L0_attn_W_in, L0_attn_conv_w, L0_attn_W_x, L0_attn_W_dt,
           L0_attn_b_dt, L0_attn_A_log, L0_attn_W_out, L0_ffn_Wr, L0_ffn_W1,
           L0_ffn_b1, L0_ffn_W2, L0_ffn_b2):
    from concourse.bass_utils import run_bass_kernel_spmd
    x = np.ascontiguousarray(np.asarray(L0_attn_x, dtype=np.float32))
    W_in = np.asarray(L0_attn_W_in, dtype=np.float32)
    conv_w = np.asarray(L0_attn_conv_w, dtype=np.float32)
    W_x = np.asarray(L0_attn_W_x, dtype=np.float32)
    W_dt = np.asarray(L0_attn_W_dt, dtype=np.float32)
    b_dt = np.asarray(L0_attn_b_dt, dtype=np.float32)
    A_log = np.asarray(L0_attn_A_log, dtype=np.float32)
    W_out = np.asarray(L0_attn_W_out, dtype=np.float32)
    Wr = np.asarray(L0_ffn_Wr, dtype=np.float32)
    W1 = np.asarray(L0_ffn_W1, dtype=np.float32)
    b1 = np.asarray(L0_ffn_b1, dtype=np.float32)
    W2 = np.asarray(L0_ffn_W2, dtype=np.float32)
    b2 = np.asarray(L0_ffn_b2, dtype=np.float32)

    nc1 = _get("L1", build_L1)
    im1 = [host_L1_inputs(x, W_in, conv_w, W_x, Wr, c) for c in range(8)]
    r1 = run_bass_kernel_spmd(nc1, im1, core_ids=list(range(8))).results

    st_rows = []
    for c in range(8):
        dh = c % 2
        st_dev = np.asarray(r1[c]["st2_out"][dh])
        st_rows.append(st_dev.reshape(128, 16, 16).transpose(1, 0, 2).reshape(2048, 16))
    st_full = np.ascontiguousarray(np.concatenate(st_rows, axis=0))
    dtbc_all = [np.asarray(r1[c]["dtbc_out"]) for c in range(8)]

    nc2 = _get("L2", build_L2)
    im2 = [host_L2_inputs(x, st_full, np.asarray(r1[c]["xc_out"]), np.asarray(r1[c]["z_out"]),
                          dtbc_all, W_dt, b_dt, A_log, W1, b1, W2, b2, c)
           for c in range(8)]
    r2 = run_bass_kernel_spmd(nc2, im2, core_ids=list(range(8))).results

    gated_all = [np.asarray(r2[c]["gated_out"]) for c in range(8)]
    dense_all = [np.asarray(r2[c]["dense"]) for c in range(8)]

    nc3 = _get("L3", build_L3)
    im3 = [host_L3_inputs(x, gated_all, dense_all, W_out, c) for c in range(8)]
    r3 = run_bass_kernel_spmd(nc3, im3, core_ids=list(range(8))).results

    out = np.concatenate([np.asarray(r3[c]["out"]) for c in range(8)], axis=0)
    return out.reshape(4, 4096, 1024)


# version 07
# speedup vs baseline: 1.0120x; 1.0120x over previous
from contextlib import ExitStack

import numpy as np
import concourse.bass as bass
import concourse.mybir as mybir

F32 = mybir.dt.float32
BF16 = mybir.dt.bfloat16
I32 = mybir.dt.int32
U32 = mybir.dt.uint32
ACT = mybir.ActivationFunctionType
ALU = mybir.AluOpType

S, D, DHALF, NST, E, T, KCAP = 4096, 1024, 512, 16, 8, 16384, 4096
EPS = 1e-5
ENGS = ("sync", "vector", "scalar", "tensor", "gpsimd")


class B:

    def __init__(self, nc):
        self.nc = nc
        self.es = ExitStack()
        self.ops = {e: [] for e in ENGS}
        self._sems = {}
        self._barn = 0

    def sem(self, name):
        if name not in self._sems:
            self._sems[name] = [self.es.enter_context(self.nc.semaphore(name)), 0]
        return self._sems[name]

    def n(self, name):
        return self.sem(name)[1]

    def q(self, eng, emit, wait=(), inc=()):
        if not getattr(self, "on", True):
            return
        waits = []
        for s, v in wait:
            h, cnt = self.sem(s)
            waits.append((h, cnt if v is None else v))
        incs = []
        for s, k in inc:
            rec = self.sem(s)
            incs.append((rec[0], k))
            rec[1] += k

        def fn(E, emit=emit, waits=waits, incs=incs):
            for h, v in waits:
                E.wait_ge(h, v)
            ins = emit(E)
            for h, k in incs:
                ins.then_inc(h, k)
        self.ops[eng].append(fn)

    def barrier(self):
        name = f"bar{self._barn}"
        self._barn += 1
        for e in ENGS:
            self.q(e, lambda E: E.nop(), inc=[(name, 1)])
        for e in ENGS:
            self.q(e, lambda E: E.nop(), wait=[(name, 5)])

    def sbuf(self, name, shape, dtype, arena=None):
        tgt = arena if arena is not None else self.es
        return tgt.enter_context(self.nc.sbuf_tensor(name, shape, dtype))

    def psum(self, name, shape, dtype=F32):
        return self.es.enter_context(self.nc.psum_tensor(name, shape, dtype))

    def finish(self):
        with self.nc.Block() as block:
            for e in ENGS:
                lst = self.ops[e]
                if lst:
                    getattr(block, e)(lambda E, lst=lst: [f(E) for f in lst])
        self.es.close()
        return self.nc


def build_L1(upto=99, dbg=False):
    nc = bass.Bass("TRN2", target_bir_lowering=False, detect_race_conditions=False)
    xb = nc.dram_tensor("xb", [S, D], F32, kind="ExternalInput").ap()
    winT = nc.dram_tensor("winT", [D, 1024], F32, kind="ExternalInput").ap()
    convw = nc.dram_tensor("convw", [128, 16], F32, kind="ExternalInput").ap()
    wxT8 = nc.dram_tensor("wxT8", [128, 128], F32, kind="ExternalInput").ap()
    wr8 = nc.dram_tensor("wr8", [128, 64], F32, kind="ExternalInput").ap()
    ident = nc.dram_tensor("ident", [128, 128], BF16, kind="ExternalInput").ap()
    identf = nc.dram_tensor("identf", [128, 128], F32, kind="ExternalInput").ap()
    cst8 = nc.dram_tensor("cst8", [128, 8], F32, kind="ExternalInput").ap()

    st2_out = nc.dram_tensor("st2_out", [2, 128, 256], F32, kind="ExternalOutput").ap()
    stat_row = nc.dram_tensor("stat_row", [2, S], F32, kind="Internal").ap()
    z_out = nc.dram_tensor("z_out", [128, 4 * S], BF16, kind="ExternalOutput").ap()
    xc_out = nc.dram_tensor("xc_out", [128, 4 * S], BF16, kind="ExternalOutput").ap()
    dtbc_out = nc.dram_tensor("dtbc_out", [32, S], F32, kind="ExternalOutput").ap()
    if dbg:
        dbg_z8 = nc.dram_tensor("dbg_z8", [8, 2048], F32, kind="ExternalOutput").ap()
        dbg_zt = nc.dram_tensor("dbg_zt", [128, 128], F32, kind="ExternalOutput").ap()
        dbg_zt2 = nc.dram_tensor("dbg_zt2", [128, 128], F32, kind="ExternalOutput").ap()
        dbg_zmax = nc.dram_tensor("dbg_zmax", [128, 16], F32, kind="ExternalOutput").ap()
        dbg_zsum = nc.dram_tensor("dbg_zsum", [128, 16], F32, kind="ExternalOutput").ap()

    b = B(nc)
    q = b.q

    xnfm = b.sbuf("xnfm", [128, 8 * S], BF16)
    winbf = [b.sbuf(f"winbf{k}", [128, 1024], BF16) for k in range(8)]
    wr_sb = b.sbuf("wr_sb", [128, 64], BF16)
    wx_sb = b.sbuf("wx_sb", [128, 128], BF16)
    cw_sb = b.sbuf("cw_sb", [128, 16], F32)
    id_sb = b.sbuf("id_sb", [128, 128], BF16)
    idf_sb = b.sbuf("idf_sb", [128, 128], F32)
    c8_sb = b.sbuf("c8_sb", [128, 8], F32)
    strow_tmp = b.sbuf("strow_tmp", [32, 128], F32)
    xt = [b.sbuf(f"xt{j}", [128, D], F32) for j in range(2)]
    xbf = [b.sbuf(f"xbf{j}", [128, D], BF16) for j in range(2)]
    ssum = b.sbuf("ssum", [128, 32], F32)
    ssq = b.sbuf("ssq", [128, 32], F32)
    var_pm = b.sbuf("var_pm", [128, 32], F32)
    inv_pm = b.sbuf("inv_pm", [128, 32], F32)
    m_pm = b.sbuf("m_pm", [128, 32], F32)
    inv_bc = b.sbuf("inv_bc", [128, 2048], BF16)
    m_bc = b.sbuf("m_bc", [128, 2048], BF16)
    z8 = b.sbuf("z8", [8, 2048], F32)
    zt = b.sbuf("zt", [128, 16 * 8], F32)
    zmax = b.sbuf("zmax", [128, 16], F32)
    zsum = b.sbuf("zsum", [128, 16], F32)
    zrec = b.sbuf("zrec", [128, 16], F32)
    st_sb = b.sbuf("st_sb", [128, 256], F32)
    x_in = b.sbuf("x_in", [128, 4 * 2051], BF16)
    zhalf = b.sbuf("zhalf", [128, 4 * 2048], BF16)
    xchalf = b.sbuf("xchalf", [128, 4 * 2048], BF16)
    cacc = b.sbuf("cacc", [128, 2048], F32)
    dtbc_sb = b.sbuf("dtbc_sb", [32, 2048], F32)
    ps_tr = [b.psum(f"ps_tr{j}", [128, 128], BF16) for j in range(2)]
    ps_mm = [b.psum(f"ps_mm{j}", [128, 512], F32) for j in range(4)]
    ps_a = b.psum("ps_a", [128, 512], F32)
    ps_sc = ps_a[0:8, :]
    ps_dt = ps_a[0:32, :]
    ps_st = ps_a[0:32, 0:128]
    ps_b = b.psum("ps_b", [128, 16], F32)
    ps_sct = [ps_b[:, 0:8], ps_b[:, 8:16]]

    q("gpsimd", lambda E: E.dma_start(out=wr_sb[:], in_=wr8), inc=[("wld", 16)])
    q("sync", lambda E: E.dma_start(out=cw_sb[:], in_=convw), inc=[("cst", 16)])
    q("sync", lambda E: E.dma_start(out=id_sb[:], in_=ident), inc=[("cst", 16)])
    q("sync", lambda E: E.dma_start(out=idf_sb[:], in_=identf), inc=[("cst", 16)])
    q("sync", lambda E: E.dma_start(out=c8_sb[:], in_=cst8), inc=[("cst", 16)])
    for k in range(8):
        q("gpsimd", lambda E, k=k: E.dma_start(out=winbf[k][:], in_=winT[128 * k:128 * (k + 1), :]),
          inc=[("wld", 16)])
    q("gpsimd", lambda E: E.dma_start(out=wx_sb[:], in_=wxT8), inc=[("wld", 16)])

    for i in range(32):
        w = [("sq", i - 1)] if i >= 2 else []
        q("sync", lambda E, i=i: E.dma_start(out=xt[i % 2][:], in_=xb[128 * i:128 * (i + 1), :]),
          wait=w, inc=[(f"xld{i % 2}", 16)])
        w = [(f"xld{i % 2}", 16 * (i // 2 + 1))]
        if i >= 2:
            w.append(("trp", 8 * (i - 1)))
        q("scalar", lambda E, i=i: E.activation(out=xbf[i % 2][:], in_=xt[i % 2][:],
                                                func=ACT.Identity, accum_out=ssum[:, i:i + 1]),
          wait=w, inc=[("cast", 1)])
        q("scalar", lambda E, i=i: E.activation(out=xt[i % 2][:], in_=xt[i % 2][:],
                                                func=ACT.Square, accum_out=ssq[:, i:i + 1]),
          wait=[("cast", i + 1)], inc=[("sq", 1)])
        for dc in range(8):
            idx = 8 * i + dc
            w = [("cast", i + 1), ("cst", 64)]
            if idx >= 2:
                w.append(("trev", idx - 1))
            q("tensor", lambda E, i=i, dc=dc, idx=idx: E.transpose(
                out=ps_tr[idx % 2][:], in_=xbf[i % 2][:, 128 * dc:128 * (dc + 1)], identity=id_sb[:]),
              wait=w, inc=[("trp", 1)])
            q("vector", lambda E, i=i, dc=dc, idx=idx: E.tensor_copy(
                out=xnfm[:, dc * S + 128 * i: dc * S + 128 * (i + 1)], in_=ps_tr[idx % 2][:]),
              wait=[("trp", idx + 1)], inc=[("trev", 1)])

    q("vector", lambda E: E.tensor_scalar(out=ssum[:], in0=ssum[:], scalar1=1.0 / D, scalar2=None,
                                          op0=ALU.mult), wait=[("sq", 32), ("cast", 32)])
    q("vector", lambda E: E.tensor_scalar(out=ssq[:], in0=ssq[:], scalar1=1.0 / D, scalar2=None,
                                          op0=ALU.mult))
    q("vector", lambda E: E.tensor_tensor(out=var_pm[:], in0=ssum[:], in1=ssum[:], op=ALU.mult))
    q("vector", lambda E: E.tensor_tensor(out=var_pm[:], in0=ssq[:], in1=var_pm[:], op=ALU.subtract),
      inc=[("var", 1)])
    q("scalar", lambda E: E.activation(out=inv_pm[:], in_=var_pm[:], func=ACT.Sqrt, bias=c8_sb[:, 0:1]),
      wait=[("var", 1)], inc=[("sd", 1)])
    q("vector", lambda E: E.reciprocal(out=inv_pm[:], in_=inv_pm[:]),
      wait=[("sd", 1)], inc=[("inv", 1)])
    q("vector", lambda E: E.tensor_tensor(out=m_pm[:], in0=ssum[:], in1=inv_pm[:], op=ALU.mult),
      wait=[("inv", 1)], inc=[("mpm", 1)])
    for r, (src_pm, dep) in enumerate([(inv_pm, "inv"), (m_pm, "mpm")]):
        q("tensor", lambda E, src_pm=src_pm: E.transpose(out=ps_st, in_=src_pm[:], identity=idf_sb[:]),
          wait=[(dep, 1), ("trev", 256)] + ([("trse", 1)] if r else []), inc=[("trs", 1)])
        q("vector", lambda E: E.tensor_copy(out=strow_tmp[:], in_=ps_st),
          wait=[("trs", r + 1)] + ([("srow", 16)] if r else []), inc=[("trse", 1)])
        q("sync", lambda E, r=r: E.dma_start(
            out=stat_row[r:r + 1, :].rearrange("a (i c) -> a (i c)") if False else stat_row[r:r + 1, :],
            in_=strow_tmp[:].rearrange("i c -> (i c)").unsqueeze(0) if False else strow_tmp[:]),
          wait=[("trse", r + 1)], inc=[("srow", 16)])

    b.barrier()

    mmgrp = [0]
    for h in range(2):
        if upto < 2:
            break
        b.on = True
        t0 = 2048 * h
        q("gpsimd", lambda E, t0=t0: E.dma_start(
            out=inv_bc[:], in_=stat_row[0:1, t0:t0 + 2048].broadcast_to([128, 2048])),
          wait=[("srow", 32)], inc=[("bc", 16)])
        q("gpsimd", lambda E, t0=t0: E.dma_start(
            out=m_bc[:], in_=stat_row[1:2, t0:t0 + 2048].broadcast_to([128, 2048])),
          inc=[("bc", 16)])
        for dc in range(8):
            q("vector", lambda E, dc=dc, t0=t0: E.tensor_tensor(
                out=xnfm[:, dc * S + t0: dc * S + t0 + 2048],
                in0=xnfm[:, dc * S + t0: dc * S + t0 + 2048], in1=inv_bc[:], op=ALU.mult),
              wait=[("bc", b.n("bc"))])
            q("vector", lambda E, dc=dc, t0=t0: E.tensor_tensor(
                out=xnfm[:, dc * S + t0: dc * S + t0 + 2048],
                in0=xnfm[:, dc * S + t0: dc * S + t0 + 2048], in1=m_bc[:], op=ALU.subtract),
              inc=[("xn", 1)])
        xn_done = b.n("xn")

        b.on = upto >= 3
        for tc in range(4):
            for k in range(8):
                q("tensor", lambda E, tc=tc, k=k, t0=t0: E.matmul(
                    out=ps_sc, lhsT=wr_sb[:, 8 * k:8 * (k + 1)],
                    rhs=xnfm[:, k * S + t0 + 512 * tc: k * S + t0 + 512 * (tc + 1)],
                    start=(k == 0), stop=(k == 7)),
                  wait=([("xn", xn_done), ("z8e", b.n("z8e")), ("srow", 32), ("wld", 160)] if k == 0 else []),
                  inc=([("scm", 1)] if k == 7 else []))
            q("vector", lambda E, tc=tc: E.tensor_copy(out=z8[:, 512 * tc:512 * (tc + 1)], in_=ps_sc),
              wait=[("scm", b.n("scm"))], inc=[("z8e", 1)])
        if dbg and h == 0:
            q("gpsimd", lambda E: E.dma_start(out=dbg_z8, in_=z8[:]),
              wait=[("z8e", b.n("z8e"))], inc=[("sto", 16)])
        b.on = upto >= 3.2
        for i in range(16):
            w = [("z8e", b.n("z8e"))]
            idx = 16 * h + i
            if idx >= 1:
                w.append(("scte", idx))
            q("tensor", lambda E, i=i, idx=idx: E.transpose(
                out=ps_sct[idx % 2], in_=z8[:, 128 * i:128 * (i + 1)], identity=idf_sb[0:8, 0:8]),
              wait=w, inc=[("sct", 1)])
            q("vector", lambda E, i=i, idx=idx: E.tensor_copy(
                out=zt[:, 8 * i:8 * (i + 1)], in_=ps_sct[idx % 2]),
              wait=[("sct", idx + 1)], inc=[("scte", 1)])
        if dbg and h == 0:
            q("gpsimd", lambda E: E.dma_start(out=dbg_zt, in_=zt[:]),
              wait=[("scte", b.n("scte"))], inc=[("sto", 16)])
        b.on = upto >= 3.4
        for i in range(16):
            q("scalar", lambda E, i=i: E.activation(
                out=zt[:, 8 * i:8 * (i + 1)], in_=zt[:, 8 * i:8 * (i + 1)],
                func=ACT.Exp, accum_out=zsum[:, i:i + 1]),
              wait=[("scte", b.n("scte"))] if i == 0 else [],
              inc=[("zexp", 1)] if i == 15 else [])
        q("vector", lambda E: E.reciprocal(out=zrec[:], in_=zsum[:]),
          wait=[("zexp", b.n("zexp"))])
        q("vector", lambda E: E.drain())
        for i in range(16):
            q("vector", lambda E, i=i: E.scalar_tensor_tensor(
                out=zt[:, 8 * i:8 * (i + 1)], in0=zt[:, 8 * i:8 * (i + 1)],
                scalar=zrec[:, i:i + 1], in1=zt[:, 8 * i:8 * (i + 1)],
                op0=ALU.mult, op1=ALU.bypass),
              inc=[("smax", 1)] if i == 15 else [])
        if dbg and h == 0:
            q("gpsimd", lambda E: E.dma_start(out=dbg_zt2, in_=zt[:]),
              wait=[("smax", b.n("smax"))], inc=[("sto", 16)])
            q("gpsimd", lambda E: E.dma_start(out=dbg_zmax, in_=zmax[:]),
              wait=[("smax", b.n("smax"))], inc=[("sto", 16)])
            q("gpsimd", lambda E: E.dma_start(out=dbg_zsum, in_=zsum[:]),
              wait=[("smax", b.n("smax"))], inc=[("sto", 16)])
        b.on = upto >= 3.6
        for i in range(16):
            q("vector", lambda E, i=i: E.tensor_copy(out=st_sb[:, 16 * i:16 * i + 8],
                                                     in_=zt[:, 8 * i:8 * (i + 1)]),
              wait=([("smax", b.n("smax"))] + ([("sto", b.sem("stst")[1])] if h else []) if i == 0 else []))
            q("vector", lambda E, i=i, h=h: E.tensor_copy(
                out=st_sb[:, 16 * i + 8:16 * i + 9], in_=inv_pm[:, 16 * h + i:16 * h + i + 1]))
            q("vector", lambda E, i=i, h=h: E.tensor_copy(
                out=st_sb[:, 16 * i + 9:16 * i + 10], in_=m_pm[:, 16 * h + i:16 * h + i + 1]))
        b.on = upto >= 3.65
        q("vector", lambda E: E.memset(
            st_sb[:].rearrange("p (i c) -> p i c", c=16)[:, :, 10:16], 0.0))
        b.on = True
        q("vector", lambda E: E.nop(), inc=[("stdone", 1)])
        b.on = upto >= 3.68
        stpre = b.n("sto")
        q("gpsimd", lambda E, h=h: E.dma_start(out=st2_out[h], in_=st_sb[:]),
          wait=[("stdone", b.n("stdone"))], inc=[("sto", 16)])
        b.sem("stst")[1] = b.n("sto")
        b.on = True

        b.on = upto >= 4
        for m in range(8):
            for tc in range(4):
                g = mmgrp[0]
                mmgrp[0] += 1
                pb = ps_mm[g % 4]
                for k in range(8):
                    w = []
                    if k == 0:
                        w = [("xn", xn_done), ("wld", 160)]
                        if g >= 4:
                            w.append(("mmev", g - 3))
                    q("tensor", lambda E, m=m, tc=tc, k=k, pb=pb, t0=t0: E.matmul(
                        out=pb[:], lhsT=winbf[k][:, 128 * m:128 * (m + 1)],
                        rhs=xnfm[:, k * S + t0 + 512 * tc: k * S + t0 + 512 * (tc + 1)],
                        start=(k == 0), stop=(k == 7)),
                      wait=w, inc=([("mm", 1)] if k == 7 else []))
                if m < 4:
                    q("scalar", lambda E, m=m, tc=tc, pb=pb: E.activation(
                        out=x_in[:, m * 2051 + 3 + 512 * tc: m * 2051 + 3 + 512 * (tc + 1)],
                        in_=pb[:], func=ACT.Identity),
                      wait=[("mm", b.n("mm"))], inc=[("mmev", 1)])
                else:
                    q("scalar", lambda E, m=m, tc=tc, pb=pb: E.activation(
                        out=zhalf[:, (m - 4) * 2048 + 512 * tc: (m - 4) * 2048 + 512 * (tc + 1)],
                        in_=pb[:], func=ACT.Silu),
                      wait=[("mm", b.n("mm"))], inc=[("mmev", 1)])
        if h == 0:
            q("vector", lambda E: E.memset(
                x_in[:].rearrange("p (g t) -> p g t", t=2051)[:, :, 0:3], 0.0), inc=[("pad", 1)])
        q("gpsimd", lambda E, t0=t0: E.dma_start(
            out=z_out[:].rearrange("p (g t) -> p g t", t=S)[:, :, t0:t0 + 2048],
            in_=zhalf[:].rearrange("p (g t) -> p g t", t=2048)),
          wait=[("mmev", b.n("mmev"))], inc=[("sto", 16)])

        b.on = upto >= 5
        for g in range(4):
            base = g * 2051
            q("vector", lambda E, base=base, g=g: E.tensor_scalar(
                out=cacc[:], in0=x_in[:, base + 3: base + 3 + 2048],
                scalar1=cw_sb[:, 4 * g + 3:4 * g + 4], scalar2=None, op0=ALU.mult),
              wait=[("mmev", b.n("mmev")), ("pad", 1), ("xcs", b.n("xcs"))])
            for kk in range(3):
                inc = [("cv", 1)] if kk == 2 else []
                q("vector", lambda E, base=base, g=g, kk=kk: E.scalar_tensor_tensor(
                    out=cacc[:], in0=x_in[:, base + kk: base + kk + 2048],
                    scalar=cw_sb[:, 4 * g + kk:4 * g + kk + 1], in1=cacc[:],
                    op0=ALU.mult, op1=ALU.add), inc=inc)
            q("scalar", lambda E, g=g: E.activation(out=xchalf[:, g * 2048:(g + 1) * 2048],
                                                    in_=cacc[:], func=ACT.Silu),
              wait=[("cv", b.n("cv"))], inc=[("xcs", 1)])
        if h == 0:
            q("vector", lambda E: E.tensor_copy(
                out=x_in[:].rearrange("p (g t) -> p g t", t=2051)[:, :, 0:3],
                in_=x_in[:].rearrange("p (g t) -> p g t", t=2051)[:, :, 2048:2051]),
              wait=[("xcs", b.n("xcs"))], inc=[("pad", 1)])
        q("gpsimd", lambda E, t0=t0: E.dma_start(
            out=xc_out[:].rearrange("p (g t) -> p g t", t=S)[:, :, t0:t0 + 2048],
            in_=xchalf[:].rearrange("p (g t) -> p g t", t=2048)),
          wait=[("xcs", b.n("xcs"))], inc=[("sto", 16)])

        b.on = upto >= 6
        for tc in range(4):
            for g in range(4):
                q("tensor", lambda E, tc=tc, g=g: E.matmul(
                    out=ps_dt, lhsT=wx_sb[:, 32 * g:32 * (g + 1)],
                    rhs=xchalf[:, g * 2048 + 512 * tc: g * 2048 + 512 * (tc + 1)],
                    start=(g == 0), stop=(g == 3)),
                  wait=([("xcs", b.n("xcs")), ("dte", b.n("dte")), ("z8e", b.n("z8e"))] if g == 0 else []),
                  inc=([("dm", 1)] if g == 3 else []))
            q("vector", lambda E, tc=tc: E.tensor_copy(out=dtbc_sb[:, 512 * tc:512 * (tc + 1)],
                                                       in_=ps_dt),
              wait=[("dm", b.n("dm"))], inc=[("dte", 1)])
        q("gpsimd", lambda E, t0=t0: E.dma_start(out=dtbc_out[:, t0:t0 + 2048], in_=dtbc_sb[:]),
          wait=[("dte", b.n("dte"))], inc=[("sto", 16)])
        b.on = True
        if h == 0:
            b.barrier()
            q("vector", lambda E: E.nop(), wait=[("sto", b.n("sto"))])
            q("scalar", lambda E: E.nop(), wait=[("sto", b.n("sto"))])

    q("gpsimd", lambda E: E.nop(), wait=[("sto", b.n("sto"))])
    q("sync", lambda E: E.nop(), wait=[("sto", b.n("sto"))])
    return b


def host_L1_inputs(x, W_in, conv_w, W_x, Wr, c):
    bseq, dh = c // 2, c % 2
    ds = slice(512 * dh, 512 * (dh + 1))
    xb = np.ascontiguousarray(x[bseq])
    winT = np.ascontiguousarray(
        np.concatenate([W_in[ds].T, W_in[1024 + 512 * dh:1024 + 512 * (dh + 1)].T], axis=1))
    convw = np.ascontiguousarray(
        conv_w[ds, 0, :].reshape(4, 128, 4).transpose(1, 0, 2).reshape(128, 16))
    wxT8 = np.ascontiguousarray(
        W_x[:, ds].T.reshape(4, 128, 32).transpose(1, 0, 2).reshape(128, 128))
    wr8 = np.ascontiguousarray(Wr.reshape(8, 128, 8).transpose(1, 0, 2).reshape(128, 64))
    import ml_dtypes
    ident = np.eye(128, dtype=ml_dtypes.bfloat16)
    identf = np.eye(128, dtype=np.float32)
    cst8 = np.zeros((128, 8), np.float32)
    cst8[:, 0] = EPS
    cst8[:, 1] = 1.0
    return dict(xb=xb, winT=winT, convw=convw, wxT8=wxT8, wr8=wr8, ident=ident,
                identf=identf, cst8=cst8)



def build_L2(dbg=False, upto=99, nozf=False, nosparse=False):
    nc = bass.Bass("TRN2", target_bir_lowering=False, detect_race_conditions=False)
    import concourse.library_config as libs
    from concourse.bass_core_rust import IndirectOffsetOnAxis

    xfull = nc.dram_tensor("xfull", [T, D], F32, kind="ExternalInput").ap()
    st = nc.dram_tensor("st", [T, 16], F32, kind="ExternalInput").ap()
    xc_in = nc.dram_tensor("xc_in", [128, 4 * S], BF16, kind="ExternalInput").ap()
    z_in = nc.dram_tensor("z_in", [128, 4 * S], BF16, kind="ExternalInput").ap()
    dtbc2 = nc.dram_tensor("dtbc2", [2, 32, S], F32, kind="ExternalInput").ap()
    wdtT = nc.dram_tensor("wdtT", [16, DHALF], F32, kind="ExternalInput").ap()
    bdt8 = nc.dram_tensor("bdt8", [128, 4], F32, kind="ExternalInput").ap()
    alog8 = nc.dram_tensor("alog8", [128, 64], F32, kind="ExternalInput").ap()
    w1 = nc.dram_tensor("w1", [D, 4096], F32, kind="ExternalInput").ap()
    b1l = nc.dram_tensor("b1l", [128, 32], F32, kind="ExternalInput").ap()
    w2 = nc.dram_tensor("w2", [4096, D], F32, kind="ExternalInput").ap()
    b2l = nc.dram_tensor("b2l", [128, 8], F32, kind="ExternalInput").ap()
    iotap = nc.dram_tensor("iotap", [128, 128], F32, kind="ExternalInput").ap()
    ones_row = nc.dram_tensor("ones_row", [1, 128], F32, kind="ExternalInput").ap()
    scflat = nc.dram_tensor("scflat", [128, 128], F32, kind="ExternalInput").ap()
    ohg = nc.dram_tensor("ohg", [128, 512], F32, kind="ExternalInput").ap()
    ident = nc.dram_tensor("ident", [128, 128], BF16, kind="ExternalInput").ap()
    identf = nc.dram_tensor("identf", [128, 128], F32, kind="ExternalInput").ap()
    cst8 = nc.dram_tensor("cst8", [128, 8], F32, kind="ExternalInput").ap()

    gated_out = nc.dram_tensor("gated_out", [128, 4 * S], BF16, kind="ExternalOutput").ap()
    dense = nc.dram_tensor("dense", [T, D], BF16, kind="ExternalOutput").ap()
    if dbg:
        dbg_dt = nc.dram_tensor("dbg_dt", [128, 4 * S], F32, kind="ExternalOutput").ap()
        dbg_y = nc.dram_tensor("dbg_y", [4, 128, S], F32, kind="ExternalOutput").ap()
        dbg_idx = nc.dram_tensor("dbg_idx", [128, 32], I32, kind="ExternalOutput").ap()
        dbg_g = nc.dram_tensor("dbg_g", [128, 32], F32, kind="ExternalOutput").ap()
        dbg_xfm = nc.dram_tensor("dbg_xfm", [128, 1024], BF16, kind="ExternalOutput").ap()
        dbg_th = nc.dram_tensor("dbg_th", [1, 1], F32, kind="ExternalOutput").ap()
        dbg_dtbc = nc.dram_tensor("dbg_dtbc", [32, S], F32, kind="ExternalOutput").ap()
        dbg_dtl = nc.dram_tensor("dbg_dtl", [16, S], BF16, kind="ExternalOutput").ap()
        dbg_cnt = nc.dram_tensor("dbg_cnt", [1, 1], F32, kind="ExternalOutput").ap()

    h_buf = nc.dram_tensor("h_buf", [4096, 4096], BF16, kind="Internal").ap()
    dtbcs = nc.dram_tensor("dtbcs", [32, S], F32, kind="Internal").ap()
    spbuf = nc.dram_tensor("spbuf", [16640, 1], F32, kind="Internal").ap()

    b = B(nc)
    q = b.q

    id_sb = b.sbuf("id_sb", [128, 128], BF16)
    idf_sb = b.sbuf("idf_sb", [128, 128], F32)
    c8_sb = b.sbuf("c8_sb", [128, 8], F32)
    lo_i = b.sbuf("lo_i", [1, 1], F32)
    hi_i = b.sbuf("hi_i", [1, 1], F32)
    mid_i = b.sbuf("mid_i", [1, 1], F32)
    tmp_i = b.sbuf("tmp_i", [1, 1], F32)
    pred_f = b.sbuf("pred_f", [1, 1], F32)
    pred_i = b.sbuf("pred_i", [1, 1], F32)
    mid_bc = b.sbuf("mid_bc", [128, 1], F32)
    th_bci = b.sbuf("th_bci", [128, 1], F32)
    psums_col = b.sbuf("psums_col", [128, 1], F32)
    cnt_sb = b.sbuf("cnt_sb", [1, 1], F32)
    nf_sb = b.sbuf("nf_sb", [1, 1], U32)
    idxf = b.sbuf("idxf", [128, 32], F32)
    idxi = b.sbuf("idxi", [128, 32], I32)
    stg = b.sbuf("stg", [128, 512], F32)
    g_sb = b.sbuf("g_sb", [128, 32], F32)
    inv_g = b.sbuf("inv_g", [128, 32], F32)
    negm_g = b.sbuf("negm_g", [128, 32], F32)
    wdt_sb = b.sbuf("wdt_sb", [16, DHALF], BF16)
    bdt_sb = b.sbuf("bdt_sb", [128, 4], F32)
    A_sb = b.sbuf("A_sb", [128, 64], F32)
    dtl_bf = b.sbuf("dtl_bf", [16, S], BF16)
    b1_sb = b.sbuf("b1_sb", [128, 32], F32)
    b2_sb = b.sbuf("b2_sb", [128, 8], F32)
    ar0 = ExitStack()
    iot_sb = b.sbuf("iot_sb", [128, 128], F32, arena=ar0)
    mask_sb = b.sbuf("mask_sb", [128, 128], F32, arena=ar0)
    dtbc_a = b.sbuf("dtbc_a", [32, S], F32, arena=ar0)
    onesT = b.sbuf("onesT", [128, 128], F32, arena=ar0)
    ones_r = b.sbuf("ones_r", [1, 128], F32, arena=ar0)
    rowc = b.sbuf("rowc", [128, 1], F32, arena=ar0)
    rsum_row = b.sbuf("rsum_row", [1, 128], F32, arena=ar0)
    rcum = b.sbuf("rcum", [1, 128], F32, arena=ar0)
    base_row = b.sbuf("base_row", [1, 128], F32, arena=ar0)
    base_col = b.sbuf("base_col", [128, 1], F32, arena=ar0)
    ecum = b.sbuf("ecum", [128, 128], F32, arena=ar0)
    post = b.sbuf("post", [128, 128], F32, arena=ar0)
    post2 = b.sbuf("post2", [128, 128], F32, arena=ar0)
    posi = b.sbuf("posi", [128, 128], I32, arena=ar0)
    oh_sb = b.sbuf("oh_sb", [128, 512], F32, arena=ar0)
    dtbc_b = b.sbuf("dtbc_b", [32, S], F32, arena=ar0)
    zero2k = b.sbuf("zero2k", [128, 2048], BF16, arena=ar0)

    ps_mm = [b.psum(f"ps_mm{j}", [128, 512], F32) for j in range(4)]
    ps_tr = [b.psum(f"ps_tr{j}", [128, 128], BF16) for j in range(2)]
    ps_f = [b.psum(f"ps_f{j}", [128, 512], F32) for j in range(2)]
    ps_cnt = ps_f[1][0:1, 0:1]

    for t_, src_ in [(id_sb, ident), (idf_sb, identf), (c8_sb, cst8), (iot_sb, iotap),
                     (oh_sb, ohg), (bdt_sb, bdt8), (b1_sb, b1l), (b2_sb, b2l),
                     (dtbc_a, dtbc2[0]), (dtbc_b, dtbc2[1]), (A_sb, alog8)]:
        q("sync", lambda E, t_=t_, src_=src_: E.dma_start(out=t_[:], in_=src_), inc=[("cst", 16)])
    NCST = 11 * 16
    q("gpsimd", lambda E: E.dma_start(out=wdt_sb[:], in_=wdtT), inc=[("wdtl", 16)])
    q("vector", lambda E: E.memset(zero2k[:], 0.0), inc=[("zz", 1)])
    df = dense.rearrange("a b -> (a b)")
    b.on = not nozf
    for i in range(64):
        q("gpsimd", lambda E, i=i: E.dma_start(
            out=df[262144 * i:262144 * (i + 1)].rearrange("(p w) -> p w", p=128),
            in_=zero2k[:]),
          wait=[("zz", 1)] if i == 0 else [], inc=[("dzf", 16)])

    b.on = upto >= 2
    b.on = True
    scd = b.sbuf("scd", [128, 128], F32, arena=ar0)
    q("sync", lambda E: E.dma_start(out=scd[:], in_=scflat), inc=[("scdc", 16)])
    q("gpsimd", lambda E: E.load_library(libs.attn), wait=[("dzf", b.n("dzf"))] if not nozf else [])
    q("vector", lambda E: E.memset(lo_i[:], 0.0), wait=[("cst", NCST)])
    q("vector", lambda E: E.memset(hi_i[:], 1.0), inc=[("bst", 1)])
    NBIS = 40
    D_ = lambda E: E.drain()
    for it in range(NBIS):
        q("vector", lambda E: E.tensor_tensor(out=mid_i[:], in0=lo_i[:], in1=hi_i[:], op=ALU.add),
          wait=[("bst", b.n("bst"))] if it == 0 else [("bup", b.n("bup"))])
        q("vector", D_)
        q("vector", lambda E: E.tensor_scalar(out=mid_i[:], in0=mid_i[:], scalar1=0.5, scalar2=None,
                                              op0=ALU.mult), inc=[("bmid", 1)])
        q("gpsimd", lambda E: E.partition_broadcast(out_ap=mid_bc[:], in_ap=mid_i[:]),
          wait=[("bmid", b.n("bmid"))], inc=[("bbc2", 1)])
        q("vector", lambda E: E.tensor_scalar(
            out=mask_sb[:], in0=scd[:], scalar1=mid_bc[:, 0:1], scalar2=None,
            op0=ALU.is_ge),
          wait=[("bbc2", b.n("bbc2")), ("scdc", 16)], inc=[("bmask", 1)])
        q("scalar", lambda E: E.activation(out=mask_sb[:], in_=mask_sb[:], func=ACT.Abs,
                                           accum_out=psums_col[:]),
          wait=[("bmask", b.n("bmask"))], inc=[("bacc", 1)])
        q("tensor", lambda E: E.matmul(out=ps_cnt, lhsT=psums_col[:], rhs=c8_sb[:, 1:2],
                                       start=True, stop=True),
          wait=[("bacc", b.n("bacc"))], inc=[("bmm", 1)])
        q("vector", lambda E: E.tensor_copy(out=cnt_sb[:], in_=ps_cnt),
          wait=[("bmm", b.n("bmm"))])
        q("vector", D_)
        q("vector", lambda E: E.tensor_scalar(out=pred_f[:], in0=cnt_sb[:], scalar1=float(KCAP),
                                              scalar2=None, op0=ALU.is_ge))
        q("vector", D_)
        q("vector", lambda E: E.tensor_tensor(out=pred_i[:], in0=pred_f[:], in1=pred_f[:], op=ALU.mult))
        q("vector", D_)
        q("vector", lambda E: E.tensor_tensor(out=tmp_i[:], in0=mid_i[:], in1=lo_i[:], op=ALU.subtract))
        q("vector", D_)
        q("vector", lambda E: E.tensor_tensor(out=tmp_i[:], in0=tmp_i[:], in1=pred_i[:], op=ALU.mult))
        q("vector", D_)
        q("vector", lambda E: E.tensor_tensor(out=lo_i[:], in0=lo_i[:], in1=tmp_i[:], op=ALU.add))
        q("vector", D_)
        q("vector", lambda E: E.tensor_scalar(out=pred_i[:], in0=pred_i[:], scalar1=-1.0, scalar2=None,
                                              op0=ALU.mult))
        q("vector", D_)
        q("vector", lambda E: E.tensor_scalar(out=pred_i[:], in0=pred_i[:], scalar1=1.0, scalar2=None,
                                              op0=ALU.add))
        q("vector", D_)
        q("vector", lambda E: E.tensor_tensor(out=tmp_i[:], in0=hi_i[:], in1=mid_i[:], op=ALU.subtract))
        q("vector", D_)
        q("vector", lambda E: E.tensor_tensor(out=tmp_i[:], in0=tmp_i[:], in1=pred_i[:], op=ALU.mult))
        q("vector", D_)
        q("vector", lambda E: E.tensor_tensor(out=hi_i[:], in0=hi_i[:], in1=tmp_i[:], op=ALU.subtract),
          inc=[("bup", 1)])
    b.on = upto >= 2.5
    q("gpsimd", lambda E: E.partition_broadcast(out_ap=th_bci[:], in_ap=lo_i[:]),
      wait=[("bup", NBIS)], inc=[("thb", 1)])
    q("vector", lambda E: E.tensor_scalar(
        out=mask_sb[:], in0=scd[:], scalar1=th_bci[:, 0:1], scalar2=None,
        op0=ALU.is_ge),
      wait=[("thb", 1)])
    q("vector", lambda E: E.tensor_tensor(out=mask_sb[:], in0=mask_sb[:], in1=mask_sb[:],
                                          op=ALU.mult), inc=[("vb", 1)])
    if dbg:
        q("gpsimd", lambda E: E.dma_start(out=dbg_th, in_=lo_i[:]),
          wait=[("bup", b.n("bup"))], inc=[("sto", 16)])
        q("gpsimd", lambda E: E.dma_start(out=dbg_cnt, in_=cnt_sb[:]), inc=[("sto", 16)])
    b.on = upto >= 3
    q("sync", lambda E: E.dma_start(out=ones_r[:], in_=ones_row), inc=[("onr", 16)])
    q("vector", lambda E: E.memset(onesT[:], 1.0), inc=[("ons", 1)])
    q("scalar", lambda E: E.activation(out=mask_sb[:], in_=mask_sb[:], func=ACT.Identity,
                                       accum_out=rowc[:]),
      wait=[("vb", b.n("vb"))], inc=[("rwc", 1)])
    q("tensor", lambda E: E.transpose(out=ps_f[0][0:1, 0:128], in_=rowc[:], identity=idf_sb[:]),
      wait=[("rwc", b.n("rwc")), ("cst", NCST)], inc=[("rwt", 1)])
    q("vector", lambda E: E.tensor_copy(out=rsum_row[:], in_=ps_f[0][0:1, 0:128]),
      wait=[("rwt", b.n("rwt"))])
    q("vector", lambda E: E.drain())
    q("vector", lambda E: E.tensor_tensor_scan(out=rcum[:], data0=ones_r[:], data1=rsum_row[:],
                                               initial=0.0, op0=ALU.mult, op1=ALU.add),
      wait=[("onr", 16)])
    q("vector", lambda E: E.drain())
    q("vector", lambda E: E.memset(base_row[:, 0:1], 0.0))
    q("vector", lambda E: E.tensor_copy(out=base_row[:, 1:128], in_=rcum[:, 0:127]))
    q("vector", lambda E: E.drain(), inc=[("bsr", 1)])
    q("tensor", lambda E: E.transpose(out=ps_f[1][0:128, 0:1], in_=base_row[:],
                                      identity=idf_sb[0:1, 0:1]),
      wait=[("bsr", b.n("bsr"))], inc=[("bst2", 1)])
    q("vector", lambda E: E.tensor_copy(out=base_col[:], in_=ps_f[1][0:128, 0:1]),
      wait=[("bst2", b.n("bst2"))])
    q("vector", lambda E: E.drain(), inc=[("bcl", 1)])
    q("vector", lambda E: E.tensor_tensor_scan(out=ecum[:], data0=onesT[:], data1=mask_sb[:],
                                               initial=0.0, op0=ALU.mult, op1=ALU.add),
      wait=[("ons", 1)])
    q("vector", lambda E: E.drain())
    q("vector", lambda E: E.tensor_tensor(out=ecum[:], in0=ecum[:], in1=mask_sb[:], op=ALU.subtract))
    q("vector", lambda E: E.scalar_tensor_tensor(out=post[:], in0=ecum[:], scalar=base_col[:, 0:1],
                                                 in1=ecum[:], op0=ALU.add, op1=ALU.bypass),
      wait=[("bcl", 1)])
    q("vector", lambda E: E.tensor_tensor(out=post[:], in0=post[:], in1=mask_sb[:], op=ALU.mult))
    q("vector", lambda E: E.tensor_scalar(out=post2[:], in0=mask_sb[:], scalar1=-16400.0, scalar2=None,
                                          op0=ALU.mult))
    q("vector", lambda E: E.tensor_scalar(out=post2[:], in0=post2[:], scalar1=16400.0, scalar2=None,
                                          op0=ALU.add))
    q("vector", lambda E: E.tensor_tensor(out=post[:], in0=post[:], in1=post2[:], op=ALU.add))
    q("vector", lambda E: E.drain())
    q("vector", lambda E: E.tensor_copy(out=posi[:], in_=post[:]))
    q("vector", lambda E: E.drain(), inc=[("posd", 1)])
    for j in range(128):
        q("gpsimd", lambda E, j=j: E.indirect_dma_start(
            out=spbuf, out_offset=IndirectOffsetOnAxis(posi[:, j:j + 1], 0),
            in_=iot_sb[:, j:j + 1], in_offset=None),
          wait=[("posd", b.n("posd"))] if j == 0 else [], inc=[("spsc", 16)])
    q("sync", lambda E: E.dma_start(
        out=idxf[:], in_=spbuf[0:4096, 0:1].rearrange("(p c) a -> p (c a)", p=128)),
      wait=[("spsc", b.n("spsc"))], inc=[("idxl", 16)])
    q("vector", lambda E: E.tensor_copy(out=idxi[:], in_=idxf[:]),
      wait=[("idxl", 16)], inc=[("idxd", 1)])
    b.on = upto >= 3.6
    for j in range(32):
        q("gpsimd", lambda E, j=j: E.indirect_dma_start(
            out=stg[:, 16 * j:16 * (j + 1)], out_offset=None,
            in_=st, in_offset=IndirectOffsetOnAxis(idxi[:, j:j + 1], 0)),
          wait=[("idxd", b.n("idxd"))] if j == 0 else [], inc=[("stgd", 16)])
    b.on = upto >= 3.8
    q("vector", lambda E: E.tensor_tensor(out=oh_sb[:], in0=stg[:], in1=oh_sb[:], op=ALU.mult),
      wait=[("stgd", b.n("stgd"))], inc=[("ohm", 1)])
    for j in range(32):
        q("scalar", lambda E, j=j: E.activation(out=oh_sb[:, 16 * j:16 * (j + 1)],
                                                in_=oh_sb[:, 16 * j:16 * (j + 1)],
                                                func=ACT.Identity, accum_out=g_sb[:, j:j + 1]),
          wait=[("ohm", 1)] if j == 0 else [], inc=[("gdone", 1)] if j == 31 else [])
        q("vector", lambda E, j=j: E.tensor_copy(out=inv_g[:, j:j + 1], in_=stg[:, 16 * j + 8:16 * j + 9]),
          wait=[("stgd", b.n("stgd"))] if j == 0 else [])
        q("vector", lambda E, j=j: E.tensor_copy(out=negm_g[:, j:j + 1], in_=stg[:, 16 * j + 9:16 * j + 10]))
    q("vector", lambda E: E.drain())
    q("vector", lambda E: E.tensor_scalar(out=negm_g[:], in0=negm_g[:], scalar1=-1.0, scalar2=None,
                                          op0=ALU.mult), inc=[("invd", 1)])
    if dbg:
        q("gpsimd", lambda E: E.dma_start(out=dbg_idx, in_=idxi[:]), wait=[("idxd", 1)], inc=[("sto", 16)])
        q("gpsimd", lambda E: E.dma_start(out=dbg_g, in_=g_sb[:]), wait=[("gdone", 1)], inc=[("sto", 16)])

    b.on = upto >= 4
    q("vector", lambda E: E.tensor_tensor(out=dtbc_a[:], in0=dtbc_a[:], in1=dtbc_b[:], op=ALU.add),
      wait=[("cst", NCST)], inc=[("dtal", 1)])
    q("sync", lambda E: E.dma_start(out=dtbcs, in_=dtbc_a[:]), wait=[("dtal", 1)], inc=[("dtbs", 16)])
    q("vector", lambda E: E.tensor_copy(out=dtl_bf[:], in_=dtbc_a[0:16, :]),
      wait=[("dtal", 1)], inc=[("dtlc", 1)])
    q("scalar", lambda E: E.activation(out=A_sb[:], in_=A_sb[:], func=ACT.Exp),
      wait=[("cst", NCST)], inc=[("aexp", 1)])
    q("vector", lambda E: E.tensor_scalar(out=A_sb[:], in0=A_sb[:], scalar1=-1.0, scalar2=None,
                                          op0=ALU.mult), wait=[("aexp", 1)], inc=[("adone", 1)])

    b.on = True
    b.barrier()
    for e in ENGS:
        q(e, lambda E: E.nop(), wait=[(s_, b.n(s_)) for s_ in
                                      ("dzf", "sg", "sgte", "gdone", "stgd", "v16d", "dtbs", "dtlc")])
    ar0.close()

    ar1 = ExitStack()
    dt_sb = b.sbuf("dt_sb", [128, 4 * S], F32, arena=ar1)
    xc_sb = b.sbuf("xc_sb", [128, 4 * S], BF16, arena=ar1)
    sp_tmp = b.sbuf("sp_tmp", [128, 512], F32, arena=ar1)
    sp_tmp2 = b.sbuf("sp_tmp2", [128, 512], F32, arena=ar1)
    sp_v = b.sbuf("sp_v", [128, 512], F32, arena=ar1)
    a_t = [b.sbuf(f"a_t{j}", [128, S], F32, arena=ar1) for j in range(2)]
    u_t = b.sbuf("u_t", [128, S], BF16, arena=ar1)
    h_t = b.sbuf("h_t", [128, S], BF16, arena=ar1)
    y_gt = b.sbuf("y_gt", [128, S], F32, arena=ar1)
    bbc = [b.sbuf(f"bbc{j}", [128, S], BF16, arena=ar1) for j in range(2)]

    b.on = upto >= 4
    q("sync", lambda E: E.dma_start(out=xc_sb[:], in_=xc_in), inc=[("xcl", 16)])
    if dbg:
        q("gpsimd", lambda E: E.dma_start(out=dbg_dtbc, in_=dtbc_a[:]),
          wait=[("dtal", 1)], inc=[("sto", 16)])
        q("gpsimd", lambda E: E.dma_start(out=dbg_dtl, in_=dtl_bf[:]),
          wait=[("dtlc", 1)], inc=[("sto", 16)])
    for g in range(4):
        for tc in range(8):
            gi = g * 8 + tc
            pb = ps_mm[gi % 4]
            w = [("dtlc", 1), ("wdtl", 16)]
            if gi >= 4:
                w.append(("spev", gi - 3))
            q("tensor", lambda E, g=g, tc=tc, pb=pb: E.matmul(
                out=pb[:], lhsT=wdt_sb[:, 128 * g:128 * (g + 1)],
                rhs=dtl_bf[:, 512 * tc:512 * (tc + 1)], start=True, stop=True),
              wait=w, inc=[("dtm", 1)])
            q("scalar", lambda E, g=g, tc=tc, pb=pb: E.activation(
                out=sp_v[:], in_=pb[:], func=ACT.Identity, bias=bdt_sb[:, g:g + 1]),
              wait=[("dtm", b.n("dtm"))] + ([("spev", gi)] if gi >= 1 else []))
            q("scalar", lambda E, g=g, tc=tc: E.activation(
                out=dt_sb[:, g * S + 512 * tc: g * S + 512 * (tc + 1)], in_=sp_v[:],
                func=ACT.Relu))
            q("scalar", lambda E: E.activation(
                out=sp_tmp[:], in_=sp_v[:], func=ACT.Abs))
            q("scalar", lambda E: E.activation(out=sp_tmp2[:], in_=sp_tmp[:], func=ACT.Exp,
                                               scale=c8_sb[:, 2:3]))
            q("scalar", lambda E: E.activation(out=sp_tmp[:], in_=sp_tmp2[:], func=ACT.Ln,
                                               bias=c8_sb[:, 1:2]), inc=[("spln", 1)])
            q("vector", lambda E, g=g, tc=tc: E.tensor_tensor(
                out=dt_sb[:, g * S + 512 * tc: g * S + 512 * (tc + 1)],
                in0=dt_sb[:, g * S + 512 * tc: g * S + 512 * (tc + 1)], in1=sp_tmp[:], op=ALU.add),
              wait=[("spln", b.n("spln"))], inc=[("spev", 1)])
    if dbg:
        q("gpsimd", lambda E: E.dma_start(out=dbg_dt, in_=dt_sb[:]),
          wait=[("spev", 32)], inc=[("sto", 16)])

    b.on = upto >= 5
    for g in range(4):
        q("vector", lambda E: E.memset(y_gt[:], 0.0),
          wait=[("gst", b.n("gst"))], inc=[("ygz", 1)])
        for n in range(NST):
            it = g * NST + n
            q("gpsimd", lambda E, n=n, it=it: E.dma_start(
                out=bbc[it % 2][:], in_=dtbcs[16 + n:17 + n, :].broadcast_to([128, S])),
              wait=[("dtbs", 16)] + ([("umul", it - 1)] if it >= 2 else []),
              inc=[(f"bbd{it % 2}", 16)])
            b.on = upto >= 5.3
            q("scalar", lambda E, g=g, n=n, it=it: E.activation(
                out=a_t[it % 2][:], in_=dt_sb[:, g * S:(g + 1) * S], func=ACT.Exp,
                scale=A_sb[:, 16 * g + n:16 * g + n + 1]),
              wait=[("spev", 32), ("adone", 1)] + ([("scn", it - 1)] if it >= 2 else []),
              inc=[("aet", 1)])
            q("vector", lambda E, g=g, it=it: E.tensor_tensor(
                out=u_t[:], in0=bbc[it % 2][:],
                in1=xc_sb[:, g * S:(g + 1) * S], op=ALU.mult),
              wait=[(f"bbd{it % 2}", 16 * (it // 2 + 1)), ("xcl", 16), ("ygz", b.n("ygz"))],
              inc=[("umul", 1)])
            b.on = upto >= 5.5
            q("vector", lambda E, it=it: E.tensor_tensor_scan(
                out=h_t[:], data0=a_t[it % 2][:], data1=u_t[:], initial=0.0,
                op0=ALU.mult, op1=ALU.add),
              wait=[("aet", it + 1)], inc=[("scn", 1)])
            b.on = upto >= 5.5
            q("vector", lambda E: E.tensor_tensor(out=y_gt[:], in0=y_gt[:], in1=h_t[:], op=ALU.add),
              inc=[("yad", 1)])
        b.on = upto >= 5.7
        q("gpsimd", lambda E, g=g: E.dma_start(out=bbc[0][:], in_=z_in[:, g * S:(g + 1) * S]),
          wait=[("gst", b.n("gst")), ("umul", b.n("umul"))], inc=[(f"zgl{g % 2}", 16)])
        q("vector", lambda E, g=g: E.tensor_tensor(out=bbc[0][:], in0=y_gt[:], in1=bbc[0][:], op=ALU.mult),
          wait=[(f"zgl{g % 2}", 16 * (g // 2 + 1))], inc=[("gmul", 1)])
        if dbg:
            q("gpsimd", lambda E, g=g: E.dma_start(out=dbg_y[g], in_=y_gt[:]),
              wait=[("yad", b.n("yad"))], inc=[("sto", 16)])
        q("gpsimd", lambda E, g=g: E.dma_start(out=gated_out[:, g * S:(g + 1) * S], in_=bbc[0][:]),
          wait=[("gmul", b.n("gmul"))], inc=[("gst", 16)])

    b.on = True
    b.barrier()
    for e in ENGS:
        q(e, lambda E: E.nop(), wait=[("gst", b.n("gst")), ("sto", b.n("sto"))])
    ar1.close()

    ar2 = ExitStack()
    xg_bf = [b.sbuf(f"xg_bf{j}", [128, D], BF16, arena=ar2) for j in range(2)]
    xn_row = [b.sbuf(f"xn_row{j}", [128, D], BF16, arena=ar2) for j in range(2)]
    xfm = b.sbuf("xfm", [128, 8 * 4096], BF16, arena=ar2)
    wbig = [b.sbuf(f"wbig{k}", [128, 4096], BF16, arena=ar2) for k in range(8)]
    h_sb = b.sbuf("h_sb", [128, 4 * 4096], BF16, arena=ar2)
    hrd = [b.sbuf(f"hrd{j}", [128, 512], BF16, arena=ar2) for j in range(4)]
    xe_sb = b.sbuf("xe_sb", [128, 8 * 512], BF16, arena=ar2)
    ctrb = [b.sbuf(f"ctrb{j}", [128, D], BF16, arena=ar2) for j in range(2)]

    b.on = upto >= 6
    for j in range(32):
        par = j % 2
        w = [("idxd", 1)] if j < 2 else [("xaf", j - 1)]
        q("gpsimd", lambda E, j=j, par=par: E.indirect_dma_start(
            out=xg_bf[par][:], out_offset=None,
            in_=xfull, in_offset=IndirectOffsetOnAxis(idxi[:, j:j + 1], 0)),
          wait=w, inc=[(f"xgd{par}", 16)])
        w = [(f"xgd{par}", 16 * (j // 2 + 1)), ("invd", 1)]
        if j >= 2:
            w.append(("xte", 8 * (j - 1)))
        q("scalar", lambda E, j=j, par=par: E.activation(
            out=xn_row[par][:], in_=xg_bf[par][:], func=ACT.Identity,
            scale=inv_g[:, j:j + 1], bias=negm_g[:, j:j + 1]),
          wait=w, inc=[("xaf", 1)])
        for dc in range(8):
            idx2 = 8 * j + dc
            wv = [("xaf", j + 1)]
            if idx2 >= 2:
                wv.append(("xte", idx2 - 1))
            q("tensor", lambda E, j=j, dc=dc, par=par, idx2=idx2: E.transpose(
                out=ps_tr[idx2 % 2][:], in_=xn_row[par][:, 128 * dc:128 * (dc + 1)], identity=id_sb[:]),
              wait=wv, inc=[("xtp", 1)])
            q("vector", lambda E, j=j, dc=dc, idx2=idx2: E.tensor_copy(
                out=xfm[:, dc * 4096 + 128 * j: dc * 4096 + 128 * (j + 1)], in_=ps_tr[idx2 % 2][:]),
              wait=[("xtp", idx2 + 1)], inc=[("xte", 1)])
    if dbg:
        q("gpsimd", lambda E: E.dma_start(out=dbg_xfm, in_=xfm[:, 0:1024]),
          wait=[("xte", 256)], inc=[("sto", 16)])

    b.on = upto >= 7
    for k in range(8):
        q("gpsimd", lambda E, k=k: E.dma_start(out=wbig[k][:], in_=w1[128 * k:128 * (k + 1), :]),
          inc=[("w1l", 16)])

    for H in range(8):
        for m in range(4):
            for tc in range(8):
                gi = (H * 4 + m) * 8 + tc
                pb = ps_mm[gi % 4]
                for k in range(8):
                    w = []
                    if k == 0:
                        w = [("xte", 256), ("w1l", 128)]
                        if gi >= 4:
                            w.append(("mm1e", gi - 3))
                    q("tensor", lambda E, H=H, m=m, tc=tc, k=k, pb=pb: E.matmul(
                        out=pb[:], lhsT=wbig[k][:, 512 * H + 128 * m: 512 * H + 128 * (m + 1)],
                        rhs=xfm[:, k * 4096 + 512 * tc: k * 4096 + 512 * (tc + 1)],
                        start=(k == 0), stop=(k == 7)),
                      wait=w, inc=[("mm1", 1)] if k == 7 else [])
                q("scalar", lambda E, H=H, m=m, tc=tc, pb=pb: E.activation(
                    out=h_sb[:, m * 4096 + 512 * tc: m * 4096 + 512 * (tc + 1)], in_=pb[:],
                    func=ACT.Gelu, bias=b1_sb[:, H * 4 + m:H * 4 + m + 1]),
                  wait=[("mm1", b.n("mm1"))] + ([("hbs", H)] if (m == 0 and tc == 0 and H >= 1) else []),
                  inc=[("mm1e", 1)])
        q("gpsimd", lambda E, H=H: E.dma_start(
            out=h_buf[512 * H:512 * (H + 1), :].rearrange("(m p) t -> p m t", m=4),
            in_=h_sb[:].rearrange("p (m t) -> p m t", m=4)),
          wait=[("mm1e", b.n("mm1e"))], inc=[("hbs0", 16)])
        q("gpsimd", lambda E: E.nop(), wait=[("hbs0", b.n("hbs0"))], inc=[("hbs", 1)])

    for kt in range(32):
        q("gpsimd", lambda E, kt=kt: E.dma_start(
            out=wbig[kt // 4][:, 1024 * (kt % 4):1024 * (kt % 4 + 1)],
            in_=w2[128 * kt:128 * (kt + 1), :]),
          wait=[("mm1", b.n("mm1"))] if kt == 0 else [], inc=[("w2l", 16)])

    b.on = upto >= 8
    for ts in range(8):
        for half in range(2):
            gh = ts * 2 + half
            for kt in range(32):
                gi = gh * 32 + kt
                par = gi % 4
                w = [("hbs", 8), ("w2l", 512)] if gi < 4 else [("mm2", gi - 3)]
                q("sync", lambda E, ts=ts, kt=kt, par=par: E.dma_start(
                    out=hrd[par][:], in_=h_buf[128 * kt:128 * (kt + 1), 512 * ts:512 * (ts + 1)]),
                  wait=w, inc=[(f"hr{par}", 16)])
                for mi in range(4):
                    md = half * 4 + mi
                    w2_ = [(f"hr{par}", 16 * (gi // 4 + 1))]
                    if kt == 0 and mi == 0 and gh >= 1:
                        w2_.append(("mm2e", 4 * gh))
                    q("tensor", lambda E, ts=ts, kt=kt, md=md, mi=mi, par=par: E.matmul(
                        out=ps_mm[mi][:],
                        lhsT=wbig[kt // 4][:, 1024 * (kt % 4) + 128 * md: 1024 * (kt % 4) + 128 * (md + 1)],
                        rhs=hrd[par][:], start=(kt == 0), stop=(kt == 31)),
                      wait=w2_ if mi == 0 else [],
                      inc=[("mm2", 1)] if mi == 3 else [])
            for mi in range(4):
                md = half * 4 + mi
                w3 = [("mm2", b.n("mm2"))]
                if ts >= 1 and mi == 0:
                    w3.append(("cte", 32 * ts))
                q("scalar", lambda E, ts=ts, md=md, mi=mi: E.activation(
                    out=xe_sb[:, 512 * md:512 * (md + 1)], in_=ps_mm[mi][:],
                    func=ACT.Identity, bias=b2_sb[:, md:md + 1]),
                  wait=w3, inc=[("mm2e", 1)])
        for tt in range(4):
            j = 4 * ts + tt
            cpar = j % 2
            for md in range(8):
                idx3 = j * 8 + md
                w3 = [("mm2e", b.n("mm2e"))]
                if idx3 >= 2:
                    w3.append(("cte", idx3 - 1))
                q("tensor", lambda E, md=md, tt=tt, idx3=idx3: E.transpose(
                    out=ps_tr[idx3 % 2][:], in_=xe_sb[:, 512 * md + 128 * tt: 512 * md + 128 * (tt + 1)],
                    identity=id_sb[:]),
                  wait=w3, inc=[("ctp", 1)])
                q("scalar", lambda E, j=j, md=md, idx3=idx3, cpar=cpar: E.activation(
                    out=ctrb[cpar][:, 128 * md:128 * (md + 1)], in_=ps_tr[idx3 % 2][:],
                    func=ACT.Identity, scale=g_sb[:, j:j + 1]),
                  wait=[("ctp", idx3 + 1)] + ([("csc" + str(cpar), 16 * (j // 2))] if j >= 2 else []),
                  inc=[("cte", 1)])
            q("gpsimd", lambda E, j=j, cpar=cpar: E.indirect_dma_start(
                out=dense, out_offset=IndirectOffsetOnAxis(idxi[:, j:j + 1], 0),
                in_=ctrb[cpar][:], in_offset=None),
              wait=[("cte", b.n("cte")), ("dzf", b.n("dzf"))], inc=[("csc" + str(cpar), 16)])

    b.on = True
    q("gpsimd", lambda E: E.nop(), wait=[("csc0", b.n("csc0")), ("csc1", b.n("csc1")),
                                         ("gst", b.n("gst")), ("sto", b.n("sto"))])
    q("sync", lambda E: E.nop(), wait=[("csc0", b.n("csc0")), ("csc1", b.n("csc1"))])
    q("vector", lambda E: E.nop(), wait=[("csc0", b.n("csc0")), ("csc1", b.n("csc1"))])
    ar2.close()
    return b


def host_L2_inputs(x, st_full, xc_dev, z_dev, dtbc_all, W_dt, b_dt, A_log, W1, b1, W2, b2, c):
    bseq, dh, e = c // 2, c % 2, c
    ds = slice(512 * dh, 512 * (dh + 1))
    import ml_dtypes
    partner = c ^ 1
    iotap = (128.0 * np.arange(128)[:, None] + np.arange(128)[None, :]).astype(np.float32)
    ohg = np.zeros((128, 512), np.float32)
    ohg[:, e::16] = 1.0
    cst8 = np.zeros((128, 8), np.float32)
    cst8[:, 0] = EPS; cst8[:, 1] = 1.0; cst8[:, 2] = -1.0
    return dict(
        xfull=np.ascontiguousarray(x.reshape(T, D)),
        st=st_full,
        xc_in=xc_dev, z_in=z_dev,
        dtbc2=np.stack([dtbc_all[c], dtbc_all[partner]]),
        wdtT=np.ascontiguousarray(W_dt[ds, :].T),
        bdt8=np.ascontiguousarray(b_dt[ds].reshape(4, 128).T),
        alog8=np.ascontiguousarray(A_log[ds].reshape(4, 128, 16).transpose(1, 0, 2).reshape(128, 64)),
        w1=np.ascontiguousarray(W1[e]), b1l=np.ascontiguousarray(b1[e].reshape(8, 4, 128).transpose(2, 0, 1).reshape(128, 32)),
        w2=np.ascontiguousarray(W2[e]), b2l=np.ascontiguousarray(b2[e].reshape(8, 128).T),
        iotap=iotap, ohg=ohg, ones_row=np.ones((1, 128), np.float32),
        scflat=np.ascontiguousarray(st_full[:, e].reshape(128, 128)),
        ident=np.eye(128, dtype=ml_dtypes.bfloat16), identf=np.eye(128, dtype=np.float32),
        cst8=cst8,
    )

def build_L3():
    nc = bass.Bass("TRN2", target_bir_lowering=False, detect_race_conditions=False)
    xw = nc.dram_tensor("xw", [2048, D], F32, kind="ExternalInput").ap()
    gated2 = nc.dram_tensor("gated2", [2, 128, 4 * 2048], BF16, kind="ExternalInput").ap()
    dwin = nc.dram_tensor("dwin", [8, 2048, D], BF16, kind="ExternalInput").ap()
    woT = nc.dram_tensor("woT", [D, D], F32, kind="ExternalInput").ap()
    ident = nc.dram_tensor("ident", [128, 128], BF16, kind="ExternalInput").ap()
    out = nc.dram_tensor("out", [2048, D], F32, kind="ExternalOutput").ap()

    b = B(nc)
    q = b.q
    id_sb = b.sbuf("id_sb", [128, 128], BF16)
    wo_sb = [b.sbuf(f"wo{k}", [128, 1024], BF16) for k in range(8)]
    gt_sb = b.sbuf("gt_sb", [128, 8 * 2048], BF16)
    att_sb = b.sbuf("att_sb", [128, 8 * 2048], BF16)
    acc = [b.sbuf(f"acc{j}", [128, 1024], F32) for j in range(2)]
    dw_t = [b.sbuf(f"dw{j}", [128, 1024], BF16) for j in range(4)]
    xw_t = [b.sbuf(f"xwt{j}", [128, 1024], F32) for j in range(2)]
    out_t = [b.sbuf(f"outt{j}", [128, 1024], F32) for j in range(2)]
    ps_mm = [b.psum(f"ps_mm{j}", [128, 512], F32) for j in range(4)]
    ps_tr = [b.psum(f"ps_tr{j}", [128, 128], BF16) for j in range(2)]

    q("sync", lambda E: E.dma_start(out=id_sb[:], in_=ident), inc=[("cst", 16)])
    for k in range(8):
        q("gpsimd", lambda E, k=k: E.dma_start(out=wo_sb[k][:], in_=woT[128 * k:128 * (k + 1), :]),
          inc=[("wol", 16)])
    for half in range(2):
        q("sync", lambda E, half=half: E.dma_start(
            out=gt_sb[:, half * 4 * 2048:(half + 1) * 4 * 2048], in_=gated2[half]),
          inc=[("gtl", 16)])
    for m in range(8):
        for tc in range(4):
            gi = m * 4 + tc
            pb = ps_mm[gi % 4]
            for kc in range(8):
                w = []
                if kc == 0:
                    w = [("gtl", 32), ("wol", 128), ("cst", 16)]
                    if gi >= 4:
                        w.append(("ate", gi - 3))
                q("tensor", lambda E, m=m, tc=tc, kc=kc, pb=pb: E.matmul(
                    out=pb[:], lhsT=wo_sb[kc][:, 128 * m:128 * (m + 1)],
                    rhs=gt_sb[:, kc * 2048 + 512 * tc: kc * 2048 + 512 * (tc + 1)],
                    start=(kc == 0), stop=(kc == 7)),
                  wait=w, inc=[("atm", 1)] if kc == 7 else [])
            q("scalar", lambda E, m=m, tc=tc, pb=pb: E.activation(
                out=att_sb[:, m * 2048 + 512 * tc: m * 2048 + 512 * (tc + 1)], in_=pb[:],
                func=ACT.Identity),
              wait=[("atm", b.n("atm"))], inc=[("ate", 1)])
    for tt in range(16):
        par = tt % 2
        q("sync", lambda E, tt=tt, par=par: E.dma_start(
            out=xw_t[par][:], in_=xw[128 * tt:128 * (tt + 1), :]),
          wait=[("acc0", tt - 1)] if tt >= 2 else [], inc=[(f"xwl{par}", 16)])
        q("vector", lambda E, tt=tt, par=par: E.tensor_copy(out=acc[par][:], in_=xw_t[par][:]),
          wait=[(f"xwl{par}", 16 * (tt // 2 + 1))], inc=[("acc0", 1)])
        for e in range(8):
            gdi = 8 * tt + e
            q("gpsimd", lambda E, tt=tt, e=e, gdi=gdi: E.dma_start(
                out=dw_t[gdi % 4][:], in_=dwin[e, 128 * tt:128 * (tt + 1), :]),
              wait=[("acca", gdi - 3)] if gdi >= 4 else [], inc=[(f"dwl{gdi % 4}", 16)])
            q("vector", lambda E, tt=tt, e=e, par=par, gdi=gdi: E.tensor_tensor(
                out=acc[par][:], in0=acc[par][:], in1=dw_t[gdi % 4][:], op=ALU.add),
              wait=[(f"dwl{gdi % 4}", 16 * (gdi // 4 + 1)), ("acc0", b.n("acc0"))],
              inc=[("acca", 1)])
        for m in range(8):
            idx4 = tt * 8 + m
            w4 = [("ate", 32)]
            if idx4 >= 2:
                w4.append(("atte", idx4 - 1))
            q("tensor", lambda E, tt=tt, m=m, idx4=idx4: E.transpose(
                out=ps_tr[idx4 % 2][:], in_=att_sb[:, m * 2048 + 128 * tt: m * 2048 + 128 * (tt + 1)],
                identity=id_sb[:]),
              wait=w4, inc=[("attp", 1)])
            q("vector", lambda E, tt=tt, m=m, idx4=idx4, par=par: E.tensor_tensor(
                out=acc[par][:, 128 * m:128 * (m + 1)], in0=acc[par][:, 128 * m:128 * (m + 1)],
                in1=ps_tr[idx4 % 2][:], op=ALU.add),
              wait=[("attp", idx4 + 1), ("acca", b.n("acca"))], inc=[("atte", 1)])
        q("vector", lambda E, par=par: E.tensor_copy(out=out_t[par][:], in_=acc[par][:]),
          wait=[("atte", b.n("atte"))], inc=[("outc", 1)])
        q("gpsimd", lambda E, tt=tt, par=par: E.dma_start(
            out=out[128 * tt:128 * (tt + 1), :], in_=out_t[par][:]),
          wait=[("outc", b.n("outc"))] + ([("outs", tt - 2 + 1)] if tt >= 2 else []),
          inc=[("outs", 16)])
    q("gpsimd", lambda E: E.nop(), wait=[("outs", 16 * 16)])
    q("sync", lambda E: E.nop(), wait=[("outs", 16 * 16)])
    return b


def host_L3_inputs(x, gated_all, dense_all, W_out, c):
    import ml_dtypes
    bq, dh = c // 2, c % 2
    w0 = 2048 * c
    xw = np.ascontiguousarray(x.reshape(T, D)[w0:w0 + 2048])
    g2 = np.zeros((2, 128, 4 * 2048), dtype=ml_dtypes.bfloat16)
    for half in range(2):
        src = np.asarray(gated_all[2 * bq + half]).reshape(128, 4, 4096)
        g2[half] = np.ascontiguousarray(src[:, :, 2048 * dh:2048 * (dh + 1)]).reshape(128, 4 * 2048)
    dwin = np.stack([np.asarray(dense_all[e])[w0:w0 + 2048] for e in range(8)])
    woT = np.ascontiguousarray(W_out.T)
    return dict(xw=xw, gated2=g2, dwin=np.ascontiguousarray(dwin), woT=woT,
                ident=np.eye(128, dtype=ml_dtypes.bfloat16))


_CACHE = {}

def _get(name, builder):
    if name not in _CACHE:
        b = builder()
        nc = b.finish()
        from concourse.library_overlay import lower_extended_insts
        lower_extended_insts(nc)
        _CACHE[name] = nc
    return _CACHE[name]


def kernel(L0_attn_x, L0_attn_W_in, L0_attn_conv_w, L0_attn_W_x, L0_attn_W_dt,
           L0_attn_b_dt, L0_attn_A_log, L0_attn_W_out, L0_ffn_Wr, L0_ffn_W1,
           L0_ffn_b1, L0_ffn_W2, L0_ffn_b2):
    from concourse.bass_utils import run_bass_kernel_spmd
    x = np.ascontiguousarray(np.asarray(L0_attn_x, dtype=np.float32))
    W_in = np.asarray(L0_attn_W_in, dtype=np.float32)
    conv_w = np.asarray(L0_attn_conv_w, dtype=np.float32)
    W_x = np.asarray(L0_attn_W_x, dtype=np.float32)
    W_dt = np.asarray(L0_attn_W_dt, dtype=np.float32)
    b_dt = np.asarray(L0_attn_b_dt, dtype=np.float32)
    A_log = np.asarray(L0_attn_A_log, dtype=np.float32)
    W_out = np.asarray(L0_attn_W_out, dtype=np.float32)
    Wr = np.asarray(L0_ffn_Wr, dtype=np.float32)
    W1 = np.asarray(L0_ffn_W1, dtype=np.float32)
    b1 = np.asarray(L0_ffn_b1, dtype=np.float32)
    W2 = np.asarray(L0_ffn_W2, dtype=np.float32)
    b2 = np.asarray(L0_ffn_b2, dtype=np.float32)

    nc1 = _get("L1", build_L1)
    im1 = [host_L1_inputs(x, W_in, conv_w, W_x, Wr, c) for c in range(8)]
    r1 = run_bass_kernel_spmd(nc1, im1, core_ids=list(range(8))).results

    st_rows = []
    for c in range(8):
        dh = c % 2
        st_dev = np.asarray(r1[c]["st2_out"][dh])
        st_rows.append(st_dev.reshape(128, 16, 16).transpose(1, 0, 2).reshape(2048, 16))
    st_full = np.ascontiguousarray(np.concatenate(st_rows, axis=0))
    dtbc_all = [np.asarray(r1[c]["dtbc_out"]) for c in range(8)]

    nc2 = _get("L2", build_L2)
    im2 = [host_L2_inputs(x, st_full, np.asarray(r1[c]["xc_out"]), np.asarray(r1[c]["z_out"]),
                          dtbc_all, W_dt, b_dt, A_log, W1, b1, W2, b2, c)
           for c in range(8)]
    r2 = run_bass_kernel_spmd(nc2, im2, core_ids=list(range(8))).results

    gated_all = [np.asarray(r2[c]["gated_out"]) for c in range(8)]
    dense_all = [np.asarray(r2[c]["dense"]) for c in range(8)]

    nc3 = _get("L3", build_L3)
    im3 = [host_L3_inputs(x, gated_all, dense_all, W_out, c) for c in range(8)]
    r3 = run_bass_kernel_spmd(nc3, im3, core_ids=list(range(8))).results

    out = np.concatenate([np.asarray(r3[c]["out"]) for c in range(8)], axis=0)
    return out.reshape(4, 4096, 1024)
